# version 09
# baseline (speedup 1.0000x reference)
import numpy as np
import concourse.bass as bass
import concourse.mybir as mybir
from concourse.bass_utils import run_bass_kernel_spmd

F32 = mybir.dt.float32
I32 = mybir.dt.int32
AF = mybir.ActivationFunctionType
ALU = mybir.AluOpType
AX = mybir.AxisListType

D = 1024
LAT = 128
HPC = 8
E = 16
FP = 256
DC = D // 128
EPS = 1e-5
NEG = -1e30

ENGS = ("sync", "gpsimd", "tensor", "vector", "scalar")
SKIP = set()


class Prog:

    def __init__(self, nc):
        self.nc = nc
        self.q = {e: [] for e in ENGS}
        self.cnt = {}
        self.sems = {}
        self.pending = {}
        self.barrier = {}

    def sem(self, name):
        h = self.nc.semaphore(name).__enter__()
        self.sems[name] = h
        self.cnt[name] = 0
        self.pending[name] = 0
        self.barrier[name] = 0
        return name

    def emit(self, eng, fn):
        self.q[eng].append(fn)

    def wait(self, eng, sem, val=None):
        v = self.cnt[sem] if val is None else val
        h = self.sems[sem]
        if v > 0:
            self.emit(eng, lambda e, h=h, v=v: e.wait_ge(h, v))
            self.pending[sem] = max(self.pending[sem], v)
        return v

    def dma_barrier(self, eng_key, sem):
        if self.pending[sem] > self.barrier[sem]:
            v = self.pending[sem]
            h = self.sems[sem]
            self.emit(eng_key, lambda e, h=h, v=v: e.wait_ge(h, v))
            self.barrier[sem] = v

    def bump(self, sem, n):
        self.cnt[sem] += n
        return self.cnt[sem]

    def h(self, sem):
        return self.sems[sem]


class Seg:

    def __init__(self, h, off, p, w):
        self.h, self.off, self.p, self.w = h, off, p, w

    def __getitem__(self, idx):
        if idx == slice(None):
            return self.h[0:self.p, self.off:self.off + self.w]
        ps, cs = idx
        p0 = ps.start or 0
        p1 = self.p if ps.stop is None else ps.stop
        c0 = cs.start or 0
        c1 = self.w if cs.stop is None else cs.stop
        return self.h[p0:p1, self.off + c0:self.off + c1]


def build(T=4096, upto=None, osz=None):
    OWN = T // 2
    NT = T // 128
    CAP = T // 2
    SC512 = T // 512
    OSZ = osz or min(256, OWN)
    NSP = OWN // OSZ
    TS = min(512, OSZ)
    NTS = OSZ // TS

    nc = bass.Bass("TRN2", target_bir_lowering=False)

    d_in = {}

    def din(name, shape, dt=F32):
        d_in[name] = nc.dram_tensor(name, list(shape), dt, kind="ExternalInput")
        return d_in[name]

    xT = din("xT", (D, T))
    din("Wdkv", (128, DC * LAT))
    din("Wdq", (128, DC * LAT))
    din("Wuq", (128, DC * LAT))
    din("Wuk", (128, DC * LAT))
    din("Wuv", (128, DC * LAT))
    din("Wdqmy", (128, 512))
    din("Wuqtmy", (128, 512))
    Wotmy = din("Wotmy", (128, DC * 512))
    din("WG", (128, DC * E))
    din("BGB", (128, E))
    W1 = din("W1", (E, 128, DC * FP))
    din("B1C", (128, 2 * E))
    din("B2", (E, D))
    din("GAM", (128, DC))
    din("BET", (128, DC))
    din("IDEN", (128, 128))
    din("TRIL", (128, 128))
    IOTA = din("IOTA", (16, T))
    din("TOFF", (1, 1), I32)
    W2 = din("W2", (E, 128, 2 * D))

    out_fm = nc.dram_tensor("out_fm", [D, OWN], F32, kind="ExternalOutput")

    y_src = nc.dram_tensor("y_src", [512, T], F32)
    y_g2 = [nc.dram_tensor(f"y_g2_{k}", [256, T], F32) for k in range(4)]
    x1_d = nc.dram_tensor("x1_d", [D, T], F32)
    m_d = nc.dram_tensor("m_d", [16, T], F32)
    mown_d = nc.dram_tensor("mown_d", [16, T // 2], F32)

    P = Prog(nc)
    P.marks = {}

    def mark(name):
        P.marks[name] = {e: len(P.q[e]) for e in ENGS}
    from contextlib import ExitStack
    ctx = ExitStack()
    dA = P.sem("dA")
    dG = P.sem("dG")
    sT = P.sem("sT")
    sV = P.sem("sV")
    sS = P.sem("sS")
    sC = P.sem("sC")
    dU0 = P.sem("dU0")
    dU1 = P.sem("dU1")
    sP = P.sem("sP")

    def sb(name, shape, dt=F32):
        return ctx.enter_context(nc.sbuf_tensor(name, list(shape), dt))

    def psb(name):
        return ctx.enter_context(nc.psum_tensor(name, [128, 512], F32))

    wdkv_sb = sb("wdkv", (128, DC * LAT))
    wdq_sb = sb("wdq", (128, DC * LAT))
    wuv_sb = sb("wuv", (128, DC * LAT))
    wdqmy_sb = sb("wdqmy", (128, 512))
    wuqtmy_sb = sb("wuqtmy", (128, 512))
    wg_sb = sb("wg", (128, DC * E))
    bgb_sb = sb("bgb", (128, E))
    b1c_sb = sb("b1c", (128, 2 * E))
    b2_sb = sb("b2", (16, D))
    gam_sb = sb("gam", (128, DC))
    bet_sb = sb("bet", (128, DC))
    iden_sb = sb("iden", (128, 128))
    tril_sb = sb("tril", (128, 128))
    toff_sb = sb("toff", (1, 1), I32)
    ones1_sb = sb("ones1", (1, 128))
    onesc_sb = sb("onesc", (128, 1))
    ke1_sb = sb("ke1", (128, 128))
    keff_sb = sb("keff", (128, 512))
    ve_sb = sb("ve", (128, 512))
    kefft_sb = sb("kefft", (128, 512))
    ctx_sb = sb("ctxb", (128, 128))
    sm_sb = sb("sm", (128, 16))
    row_sb = sb("row", (1, 1024))
    mub_sb = sb("mub", (128, 512))
    rsb_sb = sb("rsb", (128, 512))
    sq_sb = sb("sq", (128, 512))
    sq2_sb = sb("sq2", (128, 512))
    lgt_sb = sb("lgt", (128, E))
    mtk_sb = sb("mtk", (128, E))
    tq_sb = sb("tq", (128, 512))
    eps_sb = sb("eps", (1, 1))

    off_ckvT = 0
    off_ckv = T
    off_q1T = 2 * T
    off_KC = 3 * T
    off_qTh = 4 * T
    off_probsT = 5 * T
    off_yTh = 6 * T
    W_SC = max(T, DC * 512)
    off_xc = 7 * T
    off_s1 = 7 * T + W_SC
    off_s2 = 7 * T + 2 * W_SC
    off_mrows = 0
    off_cumsum = T
    off_zeros = 2 * T
    off_iota = 3 * T
    off_x1own = 0
    off_outs = off_x1own + 2 * DC * OSZ
    off_h = off_outs + 2 * DC * OSZ
    off_w1 = off_h + E * 2 * OSZ
    off_w2 = off_w1 + 2 * DC * FP
    off_mb = off_w2 + 2 * 2 * D
    off_mt = off_mb + 2 * OSZ
    AR = max(7 * T + 3 * W_SC, off_mt + OSZ, 4 * T)

    arena = sb("arena", (128, AR))
    ckvT_sb = Seg(arena, off_ckvT, 128, T)
    ckv_sb = Seg(arena, off_ckv, 128, T)
    q1T_sb = Seg(arena, off_q1T, 128, T)
    KC_sb = Seg(arena, off_KC, 128, T)
    qTh_sb = Seg(arena, off_qTh, 128, T)
    probsT_sb = Seg(arena, off_probsT, 128, T)
    yTh_sb = Seg(arena, off_yTh, 128, T)
    xc_sb = Seg(arena, off_xc, 128, DC * 512)
    sc3 = [Seg(arena, off_xc, 128, W_SC), Seg(arena, off_s1, 128, W_SC),
           Seg(arena, off_s2, 128, W_SC)]
    CDB = T >= DC * 512
    if CDB:
        xcC = [Seg(arena, off_xc, 128, DC * 512), Seg(arena, off_KC, 128, DC * 512)]
        scC = [Seg(arena, off_s1, 128, DC * 512), Seg(arena, off_qTh, 128, DC * 512)]
    else:
        xcC = [Seg(arena, off_xc, 128, DC * 512)] * 2
        scC = [Seg(arena, off_s1, 128, DC * 512)] * 2
    mrows_sb = Seg(arena, off_mrows, 16, T)
    cums_sb = Seg(arena, off_cumsum, 16, T)
    zers_sb = Seg(arena, off_zeros, 16, T)
    iota_sb = Seg(arena, off_iota, 16, T)
    x1own2 = [Seg(arena, off_x1own + i * DC * OSZ, 128, DC * OSZ) for i in range(2)]
    outs2 = [Seg(arena, off_outs + i * DC * OSZ, 128, DC * OSZ) for i in range(2)]
    h_sb = Seg(arena, off_h, 128, E * 2 * OSZ)
    w1_sb = [Seg(arena, off_w1 + i * DC * FP, 128, DC * FP) for i in range(2)]
    w2_sb = [Seg(arena, off_w2 + i * 2 * D, 128, 2 * D) for i in range(2)]
    mb_sb = [Seg(arena, off_mb + i * OSZ, 128, OSZ) for i in range(2)]
    mt_sb = Seg(arena, off_mt, 16, OSZ)

    ps0, ps1 = psb("ps0"), psb("ps1")
    ps2, ps3 = psb("ps2"), psb("ps3")
    ps4, ps5 = psb("ps4"), psb("ps5")
    ps6, ps7 = psb("ps6"), psb("ps7")

    def dma(eng_key, out, in_, sem=None):
        sem = sem or (dA if eng_key == "sync" else dG)
        P.dma_barrier(eng_key, sem)
        hh = P.h(sem)
        P.emit(eng_key, lambda e, o=out, i=in_, hh=hh: e.dma_start(out=o, in_=i).then_inc(hh, 16))
        return P.bump(sem, 16)

    def tinc(eng_key, build_fn, sem):
        hh = P.h(sem)
        P.emit(eng_key, lambda e, f=build_fn, hh=hh: f(e).then_inc(hh, 1))
        return P.bump(sem, 1)

    def raw(eng_key, build_fn):
        P.emit(eng_key, build_fn)

    def vself(v):
        raw("vector", lambda e, v=v: e.wait_ge(P.h(sV), v))

    STATICS = [("Wdkv", wdkv_sb), ("Wdq", wdq_sb), ("Wuv", wuv_sb),
               ("Wdqmy", wdqmy_sb),
               ("Wuqtmy", wuqtmy_sb), ("WG", wg_sb), ("BGB", bgb_sb),
               ("B1C", b1c_sb), ("B2", b2_sb), ("GAM", gam_sb),
               ("BET", bet_sb), ("IDEN", iden_sb), ("TRIL", tril_sb),
               ("TOFF", toff_sb)]
    for name, dst in STATICS:
        sh = d_in[name].shape
        dma("sync", dst[0:sh[0], 0:sh[1]], d_in[name][0:sh[0], 0:sh[1]])
    ld0 = P.cnt[dA]
    P.wait("vector", dA, ld0)
    raw("vector", lambda e: e.memset(ones1_sb[:], 1.0))
    raw("vector", lambda e: e.memset(eps_sb[:], EPS))
    v_const = tinc("vector", lambda e: e.memset(onesc_sb[:], 1.0), sV)

    REG = {}

    def _init_regs(e):
        REG["toff"] = e.alloc_register("r_toff")
        REG["tmp"] = e.alloc_register("r_tmp")
        e.wait_ge(P.h(dA), ld0)
        e.reg_load(REG["toff"], toff_sb[:1, :1])
    P.emit("gpsimd", _init_regs)

    mark("loads")
    P.wait("tensor", dA, ld0)
    P.wait("tensor", sV, v_const)

    dma("sync", xc_sb[0:128, 0:DC * LAT], d_in["Wuq"][0:128, 0:DC * LAT])
    dma("sync", xc_sb[0:128, DC * LAT:2 * DC * LAT], d_in["Wuk"][0:128, 0:DC * LAT])
    P.wait("tensor", dA, P.cnt[dA])
    for dc in range(DC):
        fn = lambda e, dc=dc: e.matmul(
            ps5[:, 0:128], xc_sb[0:128, dc * LAT:(dc + 1) * LAT],
            xc_sb[0:128, DC * LAT + dc * LAT: DC * LAT + (dc + 1) * LAT],
            start=(dc == 0), stop=(dc == DC - 1))
        if dc < DC - 1:
            raw("tensor", fn)
        else:
            t = tinc("tensor", fn, sT)
    P.wait("vector", sT, t)
    v_ke1 = tinc("vector", lambda e: e.tensor_copy(ke1_sb[:], ps5[:, 0:128]), sV)

    P.wait("tensor", sV, v_ke1)
    for j in range(4):
        t = tinc("tensor", lambda e, j=j: e.matmul(
            ps5[:, 128:256], wdqmy_sb[:, j * 128:(j + 1) * 128], ke1_sb[:],
            start=True, stop=True), sT)
        P.wait("vector", sT, t)
        v = tinc("vector", lambda e, j=j: e.tensor_copy(
            keff_sb[:, j * 128:(j + 1) * 128], ps5[:, 128:256]), sV)
        P.wait("tensor", sV, v)
    for j in range(4):
        t = tinc("tensor", lambda e, j=j: e.transpose(
            ps6[:, 0:128], keff_sb[:, j * 128:(j + 1) * 128], iden_sb[:]), sT)
        P.wait("vector", sT, t)
        v = tinc("vector", lambda e, j=j: e.tensor_copy(
            kefft_sb[:, j * 128:(j + 1) * 128], ps6[:, 0:128]), sV)
        P.wait("tensor", sV, v)

    P.wait("sync", sT, P.cnt[sT])
    for dc in range(DC):
        dma("sync", xc_sb[0:128, dc * 512:(dc + 1) * 512],
            Wotmy[0:128, dc * 512:(dc + 1) * 512])
    P.wait("tensor", dA, P.cnt[dA])
    for dc in range(DC):
        fn = lambda e, dc=dc: e.matmul(
            ps5[:, 0:512], wuv_sb[:, dc * LAT:(dc + 1) * LAT],
            xc_sb[0:128, dc * 512:(dc + 1) * 512], start=(dc == 0), stop=(dc == DC - 1))
        if dc < DC - 1:
            raw("tensor", fn)
        else:
            t = tinc("tensor", fn, sT)
    P.wait("vector", sT, t)
    v_ve = tinc("vector", lambda e: e.tensor_copy(ve_sb[:], ps5[:, 0:512]), sV)

    for sc in range(SC512):
        P.wait("sync", sT, P.cnt[sT])
        for dc in range(DC):
            dma("sync", xc_sb[0:128, dc * 512:(dc + 1) * 512],
                xT[dc * 128:(dc + 1) * 128, sc * 512:(sc + 1) * 512])
        P.wait("tensor", dA, P.cnt[dA])
        if sc == 0:
            P.wait("tensor", sV, v_ve)
        for dc in range(DC):
            fn = lambda e, dc=dc: e.matmul(
                ps5[:, 0:512], wdkv_sb[:, dc * LAT:(dc + 1) * LAT],
                xc_sb[0:128, dc * 512:(dc + 1) * 512], start=(dc == 0), stop=(dc == DC - 1))
            if dc < DC - 1:
                raw("tensor", fn)
            else:
                t = tinc("tensor", fn, sT)
        for dc in range(DC):
            fn = lambda e, dc=dc: e.matmul(
                ps4[:, 0:512], wdq_sb[:, dc * LAT:(dc + 1) * LAT],
                xc_sb[0:128, dc * 512:(dc + 1) * 512], start=(dc == 0), stop=(dc == DC - 1))
            if dc < DC - 1:
                raw("tensor", fn)
            else:
                t2 = tinc("tensor", fn, sT)
        P.wait("vector", sT, t)
        v = tinc("vector", lambda e, sc=sc: e.tensor_copy(
            ckvT_sb[0:128, sc * 512:(sc + 1) * 512], ps5[:, 0:512]), sV)
        P.wait("vector", sT, t2)
        v2 = tinc("vector", lambda e, sc=sc: e.tensor_copy(
            q1T_sb[0:128, sc * 512:(sc + 1) * 512], ps4[:, 0:512]), sV)
        P.wait("tensor", sV, v2)
        bank = ps6 if sc % 2 == 0 else ps7
        P.wait("tensor", sS, P.cnt[sS])
        for j in range(4):
            fn = lambda e, sc=sc, j=j, bank=bank: e.transpose(
                bank[:, j * 128:(j + 1) * 128],
                ckvT_sb[0:128, sc * 512 + j * 128: sc * 512 + (j + 1) * 128], iden_sb[:])
            if j < 3:
                raw("tensor", fn)
            else:
                t = tinc("tensor", fn, sT)
        P.wait("scalar", sT, t)
        tinc("scalar", lambda e, sc=sc, bank=bank: e.copy(
            ckv_sb[0:128, sc * 512:(sc + 1) * 512], bank[:, 0:512]), sS)

    mark("prologue")
    g_up = [0, 0]
    dU = [dU0, dU1]
    SBANKS = [ps0, ps1, ps2, ps5]
    BG_s = {id(b): 0 for b in SBANKS}
    BG_v = {id(b): 0 for b in SBANKS}

    def emit_head_prologue(h):
        po = 64 * (h % 2)
        for kind in range(2):
            for sc in range(SC512):
                bank = ps5 if sc % 2 == 0 else ps2
                P.wait("tensor", sS, BG_s[id(bank)])
                P.wait("tensor", sV, BG_v[id(bank)])
                if kind == 0:
                    t = tinc("tensor", lambda e, sc=sc, po=po, h=h, bank=bank: e.matmul(
                        bank[po:po + 64, 0:512], kefft_sb[:, h * 64:(h + 1) * 64],
                        ckvT_sb[0:128, sc * 512:(sc + 1) * 512],
                        start=True, stop=True, tile_position=(0, po)), sT)
                    dstf = lambda e, sc=sc, po=po, bank=bank: e.copy(
                        KC_sb[po:po + 64, sc * 512:(sc + 1) * 512], bank[po:po + 64, 0:512])
                else:
                    t = tinc("tensor", lambda e, sc=sc, po=po, h=h, bank=bank: e.matmul(
                        bank[po:po + 64, 0:512], wuqtmy_sb[:, h * 64:(h + 1) * 64],
                        q1T_sb[0:128, sc * 512:(sc + 1) * 512],
                        start=True, stop=True, tile_position=(0, po)), sT)
                    dstf = lambda e, sc=sc, po=po, bank=bank: e.copy(
                        qTh_sb[po:po + 64, sc * 512:(sc + 1) * 512], bank[po:po + 64, 0:512])
                P.wait("scalar", sT, t)
                BG_s[id(bank)] = tinc("scalar", dstf, sS)

    def emit_S(h, g, i):
        po = 64 * (h % 2)
        smo = 4 * (i % 3)
        buf = sc3[i % 3]
        L = 128 * (g + 1)
        nch = (L + 511) // 512
        if g == 0:
            P.wait("tensor", sS, P.cnt[sS])
        ch_ev = {}
        t = 0
        for sch in range(nch):
            w = min(512, L - sch * 512)
            bank = SBANKS[sch % 4]
            P.wait("tensor", sS, BG_s[id(bank)])
            P.wait("tensor", sV, BG_v[id(bank)])
            t = tinc("tensor", lambda e, sch=sch, w=w, bank=bank, po=po, g=g: e.matmul(
                bank[:, 0:w], qTh_sb[po:po + 64, g * 128:(g + 1) * 128],
                KC_sb[po:po + 64, sch * 512: sch * 512 + w],
                start=True, stop=True), sT)
            is_last = (sch == nch - 1)
            fw = w - 128 if is_last else w
            if fw > 0:
                P.wait("scalar", sT, t)
                ch_ev[sch] = tinc("scalar", lambda e, sch=sch, fw=fw, bank=bank, buf=buf: e.activation(
                    buf[0:128, sch * 512: sch * 512 + fw], bank[:, 0:fw],
                    AF.Copy, bias=0.0, scale=0.125), sS)
                BG_s[id(bank)] = ch_ev[sch]
                P.wait("vector", sS, ch_ev[sch])
                tcol = smo + 3
                v = tinc("vector", lambda e, sch=sch, fw=fw, buf=buf, tcol=tcol: e.reduce_max(
                    sm_sb[:, tcol:tcol + 1], buf[0:128, sch * 512: sch * 512 + fw],
                    axis=AX.X, negate=True), sV)
                vself(v)
                if sch == 0:
                    v = tinc("vector", lambda e, smo=smo, tcol=tcol: e.tensor_copy(
                        sm_sb[:, smo:smo + 1], sm_sb[:, tcol:tcol + 1]), sV)
                else:
                    v = tinc("vector", lambda e, smo=smo, tcol=tcol: e.tensor_tensor(
                        sm_sb[:, smo:smo + 1], sm_sb[:, smo:smo + 1],
                        sm_sb[:, tcol:tcol + 1], ALU.min), sV)
                vself(v)
            else:
                ch_ev[sch] = P.cnt[sS]
        s_ev = P.cnt[sS]
        P.wait("vector", sT, t)
        P.wait("vector", sS, s_ev)
        lbank = SBANKS[(nch - 1) % 4]
        v_stt = tinc("vector", lambda e, L=L, buf=buf, bank=lbank: e.scalar_tensor_tensor(
            buf[0:128, L - 128: L], bank[:, (min(512, L - (nch - 1) * 512)) - 128: min(512, L - (nch - 1) * 512)],
            0.125, tril_sb[:], op0=ALU.mult, op1=ALU.add), sV)
        BG_v[id(lbank)] = v_stt
        vself(v_stt)
        tcol = smo + 3
        v = tinc("vector", lambda e, L=L, buf=buf, tcol=tcol: e.reduce_max(
            sm_sb[:, tcol:tcol + 1], buf[0:128, L - 128: L], axis=AX.X, negate=True), sV)
        vself(v)
        if L > 128:
            v_max = tinc("vector", lambda e, smo=smo, tcol=tcol: e.tensor_tensor(
                sm_sb[:, smo:smo + 1], sm_sb[:, smo:smo + 1],
                sm_sb[:, tcol:tcol + 1], ALU.min), sV)
        else:
            v_max = tinc("vector", lambda e, smo=smo, tcol=tcol: e.tensor_copy(
                sm_sb[:, smo:smo + 1], sm_sb[:, tcol:tcol + 1]), sV)
        vself(v_max)
        return dict(h=h, g=g, po=po, smo=smo, buf=buf, L=L, s_ev=s_ev,
                    v_stt=v_stt, v_max=v_max)

    def emit_S_tail(st):
        P.wait("scalar", sV, st["v_max"])
        P.wait("scalar", sS, P.cnt[sS])
        s_exp = tinc("scalar", lambda e, st=st: e.activation(
            st["buf"][0:128, 0:st["L"]], st["buf"][0:128, 0:st["L"]], AF.Exp,
            bias=sm_sb[:, st["smo"]:st["smo"] + 1], scale=1.0,
            accum_out=sm_sb[:, st["smo"] + 1:st["smo"] + 2]), sS)
        P.wait("vector", sS, s_exp)
        v = tinc("vector", lambda e, st=st: e.reciprocal(
            sm_sb[:, st["smo"] + 2:st["smo"] + 3], sm_sb[:, st["smo"] + 1:st["smo"] + 2]), sV)
        vself(v)
        st["v_mult"] = tinc("vector", lambda e, st=st: e.tensor_scalar_mul(
            st["buf"][0:128, 0:st["L"]], st["buf"][0:128, 0:st["L"]],
            sm_sb[:, st["smo"] + 2:st["smo"] + 3]), sV)

    def emit_C(st):
        h, g, po, buf = st["h"], st["g"], st["po"], st["buf"]
        nb = g + 1
        P.wait("tensor", sV, st["v_mult"])
        pT = {}
        for j0 in range(0, nb, 4):
            jn = min(4, nb - j0)
            bank = ps6 if (j0 // 4) % 2 == 0 else ps7
            if j0 >= 8:
                P.wait("tensor", sS, pT[j0 - 8])
            for j in range(j0, j0 + jn):
                fn = lambda e, j=j, j0=j0, bank=bank, buf=buf: e.transpose(
                    bank[:, (j - j0) * 128:(j - j0 + 1) * 128],
                    buf[0:128, j * 128:(j + 1) * 128], iden_sb[:])
                if j < j0 + jn - 1:
                    raw("tensor", fn)
                else:
                    t = tinc("tensor", fn, sT)
            P.wait("scalar", sT, t)
            pT[j0] = tinc("scalar", lambda e, j0=j0, jn=jn, bank=bank: e.copy(
                probsT_sb[0:128, j0 * 128:(j0 + jn) * 128], bank[:, 0:jn * 128]), sS)
        P.wait("tensor", sS, pT[(nb - 1) // 4 * 4])
        for j in range(nb):
            fn = lambda e, j=j, nb=nb: e.matmul(
                ps3[:, 0:128], ckv_sb[0:128, j * 128:(j + 1) * 128],
                probsT_sb[0:128, j * 128:(j + 1) * 128],
                start=(j == 0), stop=(j == nb - 1))
            if j < nb - 1:
                raw("tensor", fn)
            else:
                t = tinc("tensor", fn, sT)
        P.wait("scalar", sT, t)
        s_cx = tinc("scalar", lambda e: e.copy(ctx_sb[:], ps3[:, 0:128]), sS)
        P.wait("tensor", sS, s_cx)
        t = tinc("tensor", lambda e, h=h, po=po: e.matmul(
            ps4[po:po + 64, 0:128], ve_sb[:, h * 64:(h + 1) * 64], ctx_sb[:],
            start=True, stop=True, tile_position=(0, po)), sT)
        P.wait("vector", sT, t)
        if g == 0:
            P.wait("vector", dU[h % 2], g_up[h % 2])
        st["v_y"] = tinc("vector", lambda e, h=h, g=g, po=po: e.tensor_copy(
            yTh_sb[po:po + 64, g * 128:(g + 1) * 128], ps4[po:po + 64, 0:128]), sV)

    def upload_head(hp):
        P.wait("gpsimd", sV, P.cnt[sV])
        g_up[hp % 2] = dma(
            "gpsimd", y_src[64 * hp:64 * hp + 64, 0:T],
            yTh_sb[64 * (hp % 2):64 * (hp % 2) + 64, 0:T], sem=dU[hp % 2])
        if hp % 2 == 1:
            k = hp // 2
            P.wait("gpsimd", dU0, P.cnt[dU0])
            P.wait("gpsimd", dU1, P.cnt[dU1])
            tinc("gpsimd", lambda e, k=k: e.collective_compute(
                "AllGather", ALU.bypass,
                replica_groups=[[0, 1], [2, 3], [4, 5], [6, 7]],
                ins=[y_src[128 * k:128 * k + 128, 0:T].opt()],
                outs=[y_g2[k][0:256, 0:T].opt()]), sC)

    STATE = {"prev": None}
    steps = [(h, g) for h in range(HPC) for g in range(NT)]
    hist = []
    for i, (h, g) in enumerate(steps):
        if g == 0 and h == 0:
            emit_head_prologue(0)
        if g == NT - 2 and h + 1 < HPC:
            emit_head_prologue(h + 1)
        st = emit_S(h, g, i)
        STATE["prev"] = st
        hist.append(st)
        if i >= 2:
            pv = hist[i - 2]
            emit_C(pv)
            if pv["g"] == NT - 1:
                upload_head(pv["h"])
        if i >= 1:
            emit_S_tail(hist[i - 1])
    pv = hist[-2]
    emit_C(pv)
    if pv["g"] == NT - 1:
        upload_head(pv["h"])
    emit_S_tail(hist[-1])
    pv = hist[-1]
    emit_C(pv)
    if pv["g"] == NT - 1:
        upload_head(pv["h"])

    mark("attention")
    mark("exchange")
    P.wait("sync", sC, P.cnt[sC])
    P.wait("sync", sV, P.cnt[sV])
    P.wait("sync", sT, P.cnt[sT])
    GPD = 5
    cons_gate = [None, None]
    ld_val = [0] * (SC512 + 1)

    def c_loads(sc):
        b = (sc % 2) if CDB else 0
        if cons_gate[b] is not None:
            P.wait("sync", sV, cons_gate[b]["v"])
            P.wait("sync", sT, cons_gate[b]["t"])
            P.wait("sync", sP, cons_gate[b]["p"])
            P.wait("sync", dA, cons_gate[b]["a"])
        for dc in range(DC):
            dma("sync", xcC[b][0:128, dc * 512:(dc + 1) * 512],
                xT[dc * 128:(dc + 1) * 128, sc * 512:(sc + 1) * 512])
        for dc in range(DC):
            ysrc = y_g2[dc] if dc < 4 else y_g2[dc - 4]
            ro = 0 if dc < 4 else 128
            dma("sync", scC[b][0:128, dc * 512:(dc + 1) * 512],
                ysrc[ro:ro + 128, sc * 512:(sc + 1) * 512])
        ld_val[sc] = P.cnt[dA]
        P.pending[dA] = max(P.pending[dA], P.cnt[dA])

    def c_compute(sc):
        b = (sc % 2) if CDB else 0
        xb, sb_ = xcC[b], scC[b]
        P.wait("vector", dA, ld_val[sc])
        P.wait("gpsimd", dA, ld_val[sc])
        for dc in range(GPD):
            fn = lambda e, dc=dc: e.tensor_add(
                xb[0:128, dc * 512:(dc + 1) * 512],
                xb[0:128, dc * 512:(dc + 1) * 512],
                sb_[0:128, dc * 512:(dc + 1) * 512])
            if dc < GPD - 1:
                raw("vector", fn)
            else:
                v_z = tinc("vector", fn, sV)
        for dc in range(GPD, DC):
            fn = lambda e, dc=dc: e.tensor_add(
                xb[0:128, dc * 512:(dc + 1) * 512],
                xb[0:128, dc * 512:(dc + 1) * 512],
                sb_[0:128, dc * 512:(dc + 1) * 512])
            if dc < DC - 1:
                raw("gpsimd", fn)
            else:
                p_z = tinc("gpsimd", fn, sP)
        P.wait("scalar", sV, v_z)
        P.wait("scalar", sP, p_z)
        P.wait("tensor", sV, v_z)
        P.wait("tensor", sP, p_z)
        for dc in range(DC):
            fn = lambda e, dc=dc: e.matmul(
                ps4[0:1, 0:512], onesc_sb[:], xb[0:128, dc * 512:(dc + 1) * 512],
                start=(dc == 0), stop=(dc == DC - 1))
            if dc < DC - 1:
                raw("tensor", fn)
            else:
                tinc("tensor", fn, sT)
        sq_mm = {}
        for dc in range(DC):
            sqb = sq_sb if dc % 2 == 0 else sq2_sb
            if dc >= 2:
                P.wait("scalar", sT, sq_mm[dc - 2])
            s = tinc("scalar", lambda e, dc=dc, sqb=sqb: e.activation(
                sqb[:], xb[0:128, dc * 512:(dc + 1) * 512], AF.Square,
                bias=0.0, scale=1.0), sS)
            P.wait("tensor", sS, s)
            sq_mm[dc] = tinc("tensor", lambda e, dc=dc, sqb=sqb: e.matmul(
                ps5[0:1, 0:512], onesc_sb[:], sqb[:],
                start=(dc == 0), stop=(dc == DC - 1)), sT)
        P.wait("vector", sT, P.cnt[sT])
        raw("vector", lambda e: e.tensor_scalar_mul(row_sb[:, 0:512], ps4[0:1, 0:512], 1.0 / D))
        v = tinc("vector", lambda e: e.tensor_scalar_mul(row_sb[:, 512:1024], ps5[0:1, 0:512], 1.0 / D), sV)
        vself(v)
        v = tinc("vector", lambda e: e.tensor_tensor(
            sq_sb[0:1, 0:512], row_sb[:, 0:512], row_sb[:, 0:512], ALU.mult), sV)
        vself(v)
        v = tinc("vector", lambda e: e.tensor_sub(
            row_sb[:, 512:1024], row_sb[:, 512:1024], sq_sb[0:1, 0:512]), sV)
        P.wait("scalar", sV, v)
        s = tinc("scalar", lambda e: e.activation(
            row_sb[:, 512:1024], row_sb[:, 512:1024], AF.Sqrt, bias=eps_sb[0:1, 0:1], scale=1.0), sS)
        P.wait("vector", sS, s)
        v_rs = tinc("vector", lambda e: e.reciprocal(row_sb[:, 512:1024], row_sb[:, 512:1024]), sV)
        P.wait("tensor", sV, v_rs)
        raw("tensor", lambda e: e.matmul(ps4[:, 0:512], ones1_sb[:], row_sb[:, 0:512],
                                         start=True, stop=True))
        t = tinc("tensor", lambda e: e.matmul(ps5[:, 0:512], ones1_sb[:], row_sb[:, 512:1024],
                                              start=True, stop=True), sT)
        P.wait("vector", sT, t)
        raw("vector", lambda e: e.tensor_copy(mub_sb[:], ps4[:, 0:512]))
        v_bc = tinc("vector", lambda e: e.tensor_copy(rsb_sb[:], ps5[:, 0:512]), sV)
        vself(v_bc)
        for dc in range(GPD):
            v = tinc("vector", lambda e, dc=dc: e.tensor_sub(
                sb_[0:128, dc * 512:(dc + 1) * 512],
                xb[0:128, dc * 512:(dc + 1) * 512], mub_sb[:]), sV)
            vself(v)
            v_x1 = tinc("vector", lambda e, dc=dc: e.tensor_mul(
                sb_[0:128, dc * 512:(dc + 1) * 512],
                sb_[0:128, dc * 512:(dc + 1) * 512], rsb_sb[:]), sV)
        P.wait("gpsimd", sV, v_bc)
        for dc in range(GPD, DC):
            p = tinc("gpsimd", lambda e, dc=dc: e.tensor_sub(
                sb_[0:128, dc * 512:(dc + 1) * 512],
                xb[0:128, dc * 512:(dc + 1) * 512], mub_sb[:]), sP)
            raw("gpsimd", lambda e, p=p: e.wait_ge(P.h(sP), p))
            p_x1 = tinc("gpsimd", lambda e, dc=dc: e.tensor_mul(
                sb_[0:128, dc * 512:(dc + 1) * 512],
                sb_[0:128, dc * 512:(dc + 1) * 512], rsb_sb[:]), sP)
        P.wait("sync", sV, v_x1)
        P.wait("sync", sP, p_x1)
        for dc in range(DC):
            dma("sync", x1_d[dc * 128:(dc + 1) * 128, sc * 512:(sc + 1) * 512],
                sb_[0:128, dc * 512:(dc + 1) * 512])
        st_val = P.cnt[dA]
        P.pending[dA] = max(P.pending[dA], st_val)
        P.wait("tensor", sV, v_x1)
        P.wait("tensor", sP, p_x1)
        for tt in range(4):
            for dc in range(DC):
                fn = lambda e, tt=tt, dc=dc: e.matmul(
                    ps4[:, 0:E],
                    sb_[0:128, dc * 512 + tt * 128: dc * 512 + tt * 128 + 128],
                    wg_sb[:, dc * E:(dc + 1) * E], start=(dc == 0), stop=(dc == DC - 1))
                if dc < DC - 1:
                    raw("tensor", fn)
                else:
                    t = tinc("tensor", fn, sT)
            P.wait("vector", sT, t)
            v = tinc("vector", lambda e: e.tensor_add(lgt_sb[:], ps4[:, 0:E], bgb_sb[:]), sV)
            vself(v)
            v = tinc("vector", lambda e: e.max(out=sm_sb[:, 0:8], in_=lgt_sb[:]), sV)
            vself(v)
            v = tinc("vector", lambda e: e.tensor_copy(sm_sb[:, 8:9], sm_sb[:, 1:2]), sV)
            vself(v)
            v_m = tinc("vector", lambda e: e.tensor_scalar(
                mtk_sb[:], lgt_sb[:], sm_sb[:, 8:9], None, op0=ALU.is_ge), sV)
            P.wait("tensor", sV, v_m)
            t = tinc("tensor", lambda e: e.transpose(ps4[0:16, 0:128], mtk_sb[:], iden_sb[:]), sT)
            P.wait("vector", sT, t)
            v = tinc("vector", lambda e, sc=sc, tt=tt: e.tensor_copy(
                mrows_sb[0:16, sc * 512 + tt * 128: sc * 512 + tt * 128 + 128],
                ps4[0:16, 0:128]), sV)
            P.wait("tensor", sV, v)
        cons_gate[b] = {"v": P.cnt[sV], "t": P.cnt[sT], "p": P.cnt[sP], "a": st_val}

    if CDB:
        c_loads(0)
        for sc in range(SC512):
            if sc + 1 < SC512:
                c_loads(sc + 1)
            c_compute(sc)
    else:
        for sc in range(SC512):
            c_loads(sc)
            c_compute(sc)

    mark("lnc")
    P.wait("sync", dG, P.cnt[dG])
    dma("sync", iota_sb[:], IOTA[0:16, 0:T])
    P.wait("vector", dA, P.cnt[dA])
    v = tinc("vector", lambda e: e.memset(zers_sb[:], 0.0), sV)
    vself(v)
    v = tinc("vector", lambda e: e.reduce_sum(sm_sb[0:16, 0:1], mrows_sb[:], axis=AX.X), sV)
    vself(v)
    v = tinc("vector", lambda e: e.tensor_scalar(
        sm_sb[0:16, 1:2], sm_sb[0:16, 0:1], -1.0, float(CAP), op0=ALU.mult, op1=ALU.add), sV)
    vself(v)
    v = tinc("vector", lambda e: e.tensor_tensor_scan(
        cums_sb[:], mrows_sb[:], zers_sb[:], 0.0, op0=ALU.add, op1=ALU.add), sV)
    vself(v)
    v = tinc("vector", lambda e: e.tensor_sub(cums_sb[:], iota_sb[:], cums_sb[:]), sV)
    vself(v)
    v = tinc("vector", lambda e: e.tensor_scalar(
        cums_sb[:], cums_sb[:], sm_sb[0:16, 1:2], None, op0=ALU.is_lt), sV)
    vself(v)
    v = tinc("vector", lambda e: e.tensor_scalar(
        zers_sb[:], mrows_sb[:], -1.0, 1.0, op0=ALU.mult, op1=ALU.add), sV)
    vself(v)
    v = tinc("vector", lambda e: e.tensor_tensor(cums_sb[:], cums_sb[:], zers_sb[:], ALU.min), sV)
    vself(v)
    v_fin = tinc("vector", lambda e: e.tensor_tensor(mrows_sb[:], mrows_sb[:], cums_sb[:], ALU.max), sV)

    P.wait("gpsimd", sV, v_fin)
    dma("gpsimd", m_d[0:16, 0:T], mrows_sb[:])
    P.wait("gpsimd", dG, P.cnt[dG])
    P.wait("gpsimd", dA, P.cnt[dA])

    mark("fill")
    used_gate = [0, 0]
    w_ready = [0, 0]
    x1_gate = [None, None]
    out_gate = [0, 0]
    mt_gate = 0
    ln_prev = None

    def obank(dc):
        return [ps0, ps1, ps2, ps3][dc // 2]

    def ooff(dc):
        return (dc % 2) * OSZ

    def pull_own(k):
        par = k % 2
        P.wait("gpsimd", sT, max(mt_gate, P.cnt[sT] if k > 0 else 0))
        if k > 0:
            P.wait("gpsimd", sV, P.cnt[sV])
            P.wait("gpsimd", sS, P.cnt[sS])
        q0 = k * OSZ

        def _pull(e, q0=q0, par=par, gv=P.cnt[dG] + 16):
            e.reg_alu(REG["tmp"], REG["toff"], q0, ALU.add)
            e.dma_start(out=mt_sb[:],
                        in_=bass.AP(m_d, REG["tmp"], [[T, 16], [1, OSZ]])).then_inc(P.h(dG), 16)
            e.wait_ge(P.h(dG), gv)
            e.dma_start(out=bass.AP(arena, off_x1own + par * DC * OSZ,
                                    [[AR, 128], [OSZ, DC], [1, OSZ]]),
                        in_=bass.AP(x1_d, REG["tmp"], [[T, 128], [128 * T, DC], [1, OSZ]])
                        ).then_inc(P.h(dG), 16)
            e.dma_start(out=mown_d[0:16, 0:OSZ], in_=mt_sb[:]).then_inc(P.h(dG), 16)
        P.dma_barrier("gpsimd", dG)
        P.emit("gpsimd", _pull)
        P.bump(dG, 16 * 3)
        P.pending[dG] = max(P.pending[dG], P.cnt[dG])
        return P.cnt[dG]

    def prefetch_w(e_idx, buf):
        dma("gpsimd", w1_sb[buf][:], W1[e_idx, 0:128, 0:DC * FP])
        dma("gpsimd", w2_sb[buf][:], W2[e_idx, 0:128, 0:2 * D])

    def prefetch_mb(e_idx, buf, gown):
        P.wait("gpsimd", dG, gown)

        def _mb(e, e_idx=e_idx, buf=buf):
            e.dma_start(out=mb_sb[buf][:],
                        in_=bass.AP(mown_d, e_idx * (T // 2), [[0, 128], [1, OSZ]])
                        ).then_inc(P.h(dG), 16)
        P.dma_barrier("gpsimd", dG)
        P.emit("gpsimd", _mb)
        P.bump(dG, 16)

    def emit_ln2(src_sb, dst_sb, affine, resid_sb, v_ready, s_ready, bank_a, bank_b):
        P.wait("scalar", sV, v_ready)
        P.wait("tensor", sV, v_ready)
        if s_ready:
            P.wait("tensor", sS, s_ready)
        for dc in range(DC):
            fn = lambda e, dc=dc, bank_a=bank_a: e.matmul(
                bank_a[0:1, 0:OSZ], onesc_sb[:],
                src_sb[0:128, dc * OSZ:(dc + 1) * OSZ],
                start=(dc == 0), stop=(dc == DC - 1))
            if dc < DC - 1:
                raw("tensor", fn)
            else:
                tinc("tensor", fn, sT)
        sq_mm = {}
        for dc in range(DC):
            sqb = sq_sb if dc % 2 == 0 else sq2_sb
            if dc >= 2:
                P.wait("scalar", sT, sq_mm[dc - 2])
            s = tinc("scalar", lambda e, dc=dc, sqb=sqb: e.activation(
                sqb[:, 0:OSZ], src_sb[0:128, dc * OSZ:(dc + 1) * OSZ],
                AF.Square, bias=0.0, scale=1.0), sS)
            P.wait("tensor", sS, s)
            sq_mm[dc] = tinc("tensor", lambda e, dc=dc, sqb=sqb, bank_b=bank_b: e.matmul(
                bank_b[0:1, 0:OSZ], onesc_sb[:], sqb[:, 0:OSZ],
                start=(dc == 0), stop=(dc == DC - 1)), sT)
        P.wait("vector", sT, P.cnt[sT])
        raw("vector", lambda e, bank_a=bank_a: e.tensor_scalar_mul(
            row_sb[:, 0:OSZ], bank_a[0:1, 0:OSZ], 1.0 / D))
        v = tinc("vector", lambda e, bank_b=bank_b: e.tensor_scalar_mul(
            row_sb[:, 512:512 + OSZ], bank_b[0:1, 0:OSZ], 1.0 / D), sV)
        vself(v)
        v = tinc("vector", lambda e: e.tensor_tensor(
            sq_sb[0:1, 0:OSZ], row_sb[:, 0:OSZ], row_sb[:, 0:OSZ], ALU.mult), sV)
        vself(v)
        v = tinc("vector", lambda e: e.tensor_sub(
            row_sb[:, 512:512 + OSZ], row_sb[:, 512:512 + OSZ], sq_sb[0:1, 0:OSZ]), sV)
        P.wait("scalar", sV, v)
        s = tinc("scalar", lambda e: e.activation(
            row_sb[:, 512:512 + OSZ], row_sb[:, 512:512 + OSZ], AF.Sqrt,
            bias=eps_sb[0:1, 0:1], scale=1.0), sS)
        P.wait("vector", sS, s)
        v_rs = tinc("vector", lambda e: e.reciprocal(row_sb[:, 512:512 + OSZ], row_sb[:, 512:512 + OSZ]), sV)
        P.wait("tensor", sV, v_rs)
        raw("tensor", lambda e, bank_a=bank_a: e.matmul(
            bank_a[:, 256:256 + OSZ], ones1_sb[:], row_sb[:, 0:OSZ],
            start=True, stop=True))
        t = tinc("tensor", lambda e, bank_b=bank_b: e.matmul(
            bank_b[:, 256:256 + OSZ], ones1_sb[:], row_sb[:, 512:512 + OSZ],
            start=True, stop=True), sT)
        P.wait("vector", sT, t)
        raw("vector", lambda e, bank_a=bank_a: e.tensor_copy(mub_sb[:, 0:OSZ], bank_a[:, 256:256 + OSZ]))
        v = tinc("vector", lambda e, bank_b=bank_b: e.tensor_copy(rsb_sb[:, 0:OSZ], bank_b[:, 256:256 + OSZ]), sV)
        vself(v)
        vv = 0
        for dc in range(DC):
            v = tinc("vector", lambda e, dc=dc: e.tensor_sub(
                tq_sb[:, 0:OSZ], src_sb[0:128, dc * OSZ:(dc + 1) * OSZ],
                mub_sb[:, 0:OSZ]), sV)
            vself(v)
            v = tinc("vector", lambda e: e.tensor_mul(
                tq_sb[:, 0:OSZ], tq_sb[:, 0:OSZ], rsb_sb[:, 0:OSZ]), sV)
            vself(v)
            if affine:
                v = tinc("vector", lambda e, dc=dc: e.tensor_scalar(
                    tq_sb[:, 0:OSZ], tq_sb[:, 0:OSZ], gam_sb[:, dc:dc + 1],
                    bet_sb[:, dc:dc + 1], op0=ALU.mult, op1=ALU.add), sV)
                vself(v)
            if resid_sb is not None:
                fn = lambda e, dc=dc: e.tensor_add(
                    dst_sb[0:128, dc * OSZ:(dc + 1) * OSZ], tq_sb[:, 0:OSZ],
                    resid_sb[0:128, dc * OSZ:(dc + 1) * OSZ])
            else:
                fn = lambda e, dc=dc: e.tensor_copy(
                    dst_sb[0:128, dc * OSZ:(dc + 1) * OSZ], tq_sb[:, 0:OSZ])
            vv = tinc("vector", fn, sV)
            vself(vv)
        return vv

    LNG = {"v": 0}

    def emit_ln_tail(k):
        par = k % 2
        x1b, ob = x1own2[par], outs2[par]
        v_rdy = max(zev[par], LNG["v"])
        v_z3 = emit_ln2(ob, x1b, True, x1b, v_rdy, 0, ps6, ps7)
        v_out = emit_ln2(x1b, ob, False, None, v_z3, 0, ps6, ps7)
        P.wait("sync", sV, v_out)
        for dc in range(DC):
            dma("sync", out_fm[dc * 128:(dc + 1) * 128, k * OSZ:(k + 1) * OSZ],
                ob[0:128, dc * OSZ:(dc + 1) * OSZ])
        out_gate[par] = P.cnt[dA]
        P.pending[dA] = max(P.pending[dA], out_gate[par])
        LNG["v"] = v_out

    zev = [0, 0]
    g_own = pull_own(0)
    prefetch_w(0, 0)
    prefetch_mb(0, 0, g_own)
    w_ready[0] = P.cnt[dG]
    w2_ready = [0, 0]
    w2_used = [0, 0]
    for k in range(NSP):
        par = k % 2
        x1b, ob = x1own2[par], outs2[par]
        P.wait("tensor", sV, zev[(k - 1) % 2] if k >= 1 else 0)
        P.wait("tensor", dG, g_own)
        gel_gate = {}
        for ei in range(E):
            buf = ei % 2
            P.wait("tensor", dG, w_ready[buf])
            if ei + 1 < E:
                nbuf = (ei + 1) % 2
                if used_gate[nbuf]:
                    P.wait("gpsimd", sT, used_gate[nbuf])
                prefetch_w(ei + 1, nbuf)
                prefetch_mb(ei + 1, nbuf, g_own)
                w_ready[nbuf] = P.cnt[dG]
            for fc in range(2):
                kk = ei * 2 + fc
                bank = ps4 if kk % 2 == 0 else ps5
                P.wait("tensor", sS, gel_gate.get(kk - 2, 0))
                for dc in range(DC):
                    fn = lambda e, buf=buf, fc=fc, dc=dc, bank=bank, x1b=x1b: e.matmul(
                        bank[:, 0:OSZ],
                        w1_sb[buf][0:128, dc * FP + fc * 128: dc * FP + fc * 128 + 128],
                        x1b[0:128, dc * OSZ:(dc + 1) * OSZ],
                        start=(dc == 0), stop=(dc == DC - 1))
                    if dc < DC - 1:
                        raw("tensor", fn)
                    else:
                        t = tinc("tensor", fn, sT)
                P.wait("scalar", sT, t)
                s = tinc("scalar", lambda e, fc=fc, ei=ei, bank=bank: e.activation(
                    h_sb[0:128, (ei * 2 + fc) * OSZ:(ei * 2 + fc + 1) * OSZ], bank[:, 0:OSZ],
                    AF.Gelu, bias=b1c_sb[:, 2 * ei + fc: 2 * ei + fc + 1], scale=1.0), sS)
                gel_gate[kk] = s
                P.wait("vector", sS, s)
                P.wait("vector", dG, w_ready[buf])
                fn = lambda e, buf=buf, ei=ei, fc=fc: e.tensor_mul(
                    h_sb[0:128, (ei * 2 + fc) * OSZ:(ei * 2 + fc + 1) * OSZ],
                    h_sb[0:128, (ei * 2 + fc) * OSZ:(ei * 2 + fc + 1) * OSZ],
                    mb_sb[buf][0:128, 0:OSZ])
                if fc == 1:
                    v_h = tinc("vector", fn, sV)
                else:
                    raw("vector", fn)
            P.wait("tensor", sV, v_h)
            for dc in range(4):
                for fc in range(2):
                    fn = lambda e, buf=buf, dc=dc, fc=fc, ei=ei, st=(ei == 0 and fc == 0): e.matmul(
                        [ps0, ps1, ps2, ps3][dc][:, 0:OSZ],
                        w2_sb[buf][0:128, fc * D + dc * 128: fc * D + dc * 128 + 128],
                        h_sb[0:128, (ei * 2 + fc) * OSZ:(ei * 2 + fc + 1) * OSZ],
                        start=st, stop=False)
                    if dc == 3 and fc == 1:
                        t = tinc("tensor", fn, sT)
                    else:
                        raw("tensor", fn)
            used_gate[buf] = P.cnt[sT]
            mark(f"spe{k}_{ei}")
        v_h_all = P.cnt[sV]
        P.wait("tensor", dG, g_own)
        for dc in range(4):
            fn = lambda e, dc=dc: e.matmul(
                [ps0, ps1, ps2, ps3][dc][:, 0:OSZ], b2_sb[:, dc * 128:(dc + 1) * 128],
                mt_sb[0:16, 0:OSZ], start=False, stop=True)
            if dc < 3:
                raw("tensor", fn)
            else:
                t_b2a = tinc("tensor", fn, sT)
        P.wait("vector", sT, t_b2a)
        if out_gate[par]:
            P.wait("vector", dA, out_gate[par])
        for dc in range(4):
            fn = lambda e, dc=dc, ob=ob, x1b=x1b: e.tensor_add(
                ob[0:128, dc * OSZ:(dc + 1) * OSZ],
                [ps0, ps1, ps2, ps3][dc][:, 0:OSZ],
                x1b[0:128, dc * OSZ:(dc + 1) * OSZ])
            if dc < 3:
                raw("vector", fn)
            else:
                v_e1 = tinc("vector", fn, sV)
        vself(v_e1)
        mark(f"spp1_{k}")
        prefetch_w2only = lambda e_idx, buf: dma("gpsimd", w2_sb[buf][:], W2[e_idx, 0:128, 0:2 * D])
        if used_gate[0]:
            P.wait("gpsimd", sT, used_gate[0])
        w2_ready[0] = prefetch_w2only(0, 0)
        for ei in range(E):
            buf = ei % 2
            P.wait("tensor", dG, w2_ready[buf])
            if ei == 0:
                P.wait("tensor", sV, v_e1)
            if ei + 1 < E:
                nbuf = (ei + 1) % 2
                if w2_used[nbuf] or used_gate[nbuf]:
                    P.wait("gpsimd", sT, max(w2_used[nbuf], used_gate[nbuf]))
                w2_ready[nbuf] = prefetch_w2only(ei + 1, nbuf)
            for dc in range(4, DC):
                for fc in range(2):
                    fn = lambda e, buf=buf, dc=dc, fc=fc, ei=ei, st=(ei == 0 and fc == 0): e.matmul(
                        [ps0, ps1, ps2, ps3][dc - 4][:, 0:OSZ],
                        w2_sb[buf][0:128, fc * D + dc * 128: fc * D + dc * 128 + 128],
                        h_sb[0:128, (ei * 2 + fc) * OSZ:(ei * 2 + fc + 1) * OSZ],
                        start=st, stop=False)
                    if dc == DC - 1 and fc == 1:
                        t = tinc("tensor", fn, sT)
                    else:
                        raw("tensor", fn)
            w2_used[buf] = P.cnt[sT]
        for dc in range(4, DC):
            fn = lambda e, dc=dc: e.matmul(
                [ps0, ps1, ps2, ps3][dc - 4][:, 0:OSZ], b2_sb[:, dc * 128:(dc + 1) * 128],
                mt_sb[0:16, 0:OSZ], start=False, stop=True)
            if dc < DC - 1:
                raw("tensor", fn)
            else:
                t_b2b = tinc("tensor", fn, sT)
        mt_gate = t_b2b
        P.wait("vector", sT, t_b2b)
        for dc in range(4, DC):
            fn = lambda e, dc=dc, ob=ob, x1b=x1b: e.tensor_add(
                ob[0:128, dc * OSZ:(dc + 1) * OSZ],
                [ps0, ps1, ps2, ps3][dc - 4][:, 0:OSZ],
                x1b[0:128, dc * OSZ:(dc + 1) * OSZ])
            if dc < DC - 1:
                raw("vector", fn)
            else:
                zev[par] = tinc("vector", fn, sV)
        vself(zev[par])
        mark(f"sp{k}")
        if ln_prev is not None:
            emit_ln_tail(ln_prev)
        ln_prev = k
        if k + 1 < NSP:
            g_own = pull_own(k + 1)
            if max(w2_used[0], used_gate[0]):
                P.wait("gpsimd", sT, max(w2_used[0], used_gate[0]))
            prefetch_w(0, 0)
            prefetch_mb(0, 0, g_own)
            w_ready[0] = P.cnt[dG]
    emit_ln_tail(ln_prev)

    P.wait("sync", dA, P.cnt[dA])
    mark("all")

    if upto is not None:
        cut = P.marks[upto]
        for e in ENGS:
            P.q[e] = P.q[e][:cut[e]]

    with ctx:
        with nc.Block() as block:
            @block.sync
            def _(e):
                for f in P.q["sync"]:
                    f(e)

            @block.gpsimd
            def _(e):
                for f in P.q["gpsimd"]:
                    f(e)

            @block.tensor
            def _(e):
                for f in P.q["tensor"]:
                    f(e)

            @block.vector
            def _(e):
                for f in P.q["vector"]:
                    f(e)

            @block.scalar
            def _(e):
                for f in P.q["scalar"]:
                    f(e)

    return nc


def prep_inputs(inputs, T=4096):
    x = np.asarray(inputs["L0_attn_x"], np.float32)
    W_dq = np.asarray(inputs["L0_attn_W_dq"], np.float32)
    W_uq = np.asarray(inputs["L0_attn_W_uq"], np.float32)
    W_dkv = np.asarray(inputs["L0_attn_W_dkv"], np.float32)
    W_uk = np.asarray(inputs["L0_attn_W_uk"], np.float32)
    W_uv = np.asarray(inputs["L0_attn_W_uv"], np.float32)
    W_o = np.asarray(inputs["L0_attn_W_o"], np.float32)
    wg = np.asarray(inputs["L0_ffn_wg"], np.float32)
    bg = np.asarray(inputs["L0_ffn_bg"], np.float32)
    w1 = np.asarray(inputs["L0_ffn_w1"], np.float32)
    b1 = np.asarray(inputs["L0_ffn_b1"], np.float32)
    w2 = np.asarray(inputs["L0_ffn_w2"], np.float32)
    b2 = np.asarray(inputs["L0_ffn_b2"], np.float32)
    gamma = np.asarray(inputs["L0_ffn_gamma"], np.float32)
    beta = np.asarray(inputs["L0_ffn_beta"], np.float32)

    def cm(a):
        K = a.shape[1]
        return np.ascontiguousarray(a.reshape(DC, 128, K).transpose(1, 0, 2).reshape(128, DC * K))

    shared = {
        "Wdkv": cm(W_dkv.T), "Wdq": cm(W_dq.T), "Wuq": cm(W_uq),
        "Wuk": cm(W_uk), "Wuv": cm(W_uv), "WG": cm(wg),
        "BGB": np.ascontiguousarray(np.tile(bg[None, :], (128, 1))),
        "W1": np.ascontiguousarray(w1.reshape(E, DC, 128, FP).transpose(0, 2, 1, 3).reshape(E, 128, DC * FP)),
        "B1C": np.ascontiguousarray(b1.reshape(E, 2, 128).transpose(2, 0, 1).reshape(128, 2 * E)),
        "W2": np.ascontiguousarray(w2.reshape(E, 2, 128, D).transpose(0, 2, 1, 3).reshape(E, 128, 2 * D)),
        "B2": np.ascontiguousarray(b2),
        "GAM": np.ascontiguousarray(gamma.reshape(DC, 128).T),
        "BET": np.ascontiguousarray(beta.reshape(DC, 128).T),
        "IDEN": np.eye(128, dtype=np.float32),
        "TRIL": np.where(np.arange(128)[None, :] <= np.arange(128)[:, None],
                         0.0, NEG).astype(np.float32),
        "IOTA": np.ascontiguousarray(np.tile(np.arange(T, dtype=np.float32)[None, :], (16, 1))),
    }
    WoT = W_o.T
    WuqT = W_uq.T
    in_maps = []
    for i in range(8):
        b, p = i // 2, i % 2
        m = dict(shared)
        m["xT"] = np.ascontiguousarray(x[b].T)
        m["Wdqmy"] = np.ascontiguousarray(W_dq[:, 512 * p:512 * p + 512])
        m["Wuqtmy"] = np.ascontiguousarray(WuqT[:, 512 * p:512 * p + 512])
        m["Wotmy"] = cm(WoT[:, 512 * p:512 * p + 512])
        m["TOFF"] = np.array([[T // 2 * p]], np.int32)
        in_maps.append(m)
    return in_maps


def kernel(**inputs):
    T = int(np.asarray(inputs["L0_attn_x"]).shape[1])
    in_maps = prep_inputs(inputs, T)
    nc = build(T)
    res = run_bass_kernel_spmd(nc, in_maps, core_ids=list(range(8)))
    B = int(np.asarray(inputs["L0_attn_x"]).shape[0])
    out = np.empty((B, T, D), np.float32)
    OWN = T // 2
    for i in range(8):
        b, p = i // 2, i % 2
        out[b, OWN * p: OWN * (p + 1), :] = np.asarray(res.results[i]["out_fm"]).T
    return out


# version 10
# speedup vs baseline: 1.0042x; 1.0042x over previous
import numpy as np
import concourse.bass as bass
import concourse.mybir as mybir
from concourse.bass_utils import run_bass_kernel_spmd

F32 = mybir.dt.float32
I32 = mybir.dt.int32
AF = mybir.ActivationFunctionType
ALU = mybir.AluOpType
AX = mybir.AxisListType

D = 1024
LAT = 128
HPC = 8
E = 16
FP = 256
DC = D // 128
EPS = 1e-5
NEG = -1e30

ENGS = ("sync", "gpsimd", "tensor", "vector", "scalar")
SKIP = set()


class Prog:

    def __init__(self, nc):
        self.nc = nc
        self.q = {e: [] for e in ENGS}
        self.cnt = {}
        self.sems = {}
        self.pending = {}
        self.barrier = {}

    def sem(self, name):
        h = self.nc.semaphore(name).__enter__()
        self.sems[name] = h
        self.cnt[name] = 0
        self.pending[name] = 0
        self.barrier[name] = 0
        return name

    def emit(self, eng, fn):
        self.q[eng].append(fn)

    def wait(self, eng, sem, val=None):
        v = self.cnt[sem] if val is None else val
        h = self.sems[sem]
        if v > 0:
            self.emit(eng, lambda e, h=h, v=v: e.wait_ge(h, v))
            self.pending[sem] = max(self.pending[sem], v)
        return v

    def dma_barrier(self, eng_key, sem):
        if self.pending[sem] > self.barrier[sem]:
            v = self.pending[sem]
            h = self.sems[sem]
            self.emit(eng_key, lambda e, h=h, v=v: e.wait_ge(h, v))
            self.barrier[sem] = v

    def bump(self, sem, n):
        self.cnt[sem] += n
        return self.cnt[sem]

    def h(self, sem):
        return self.sems[sem]


class Seg:

    def __init__(self, h, off, p, w):
        self.h, self.off, self.p, self.w = h, off, p, w

    def __getitem__(self, idx):
        if idx == slice(None):
            return self.h[0:self.p, self.off:self.off + self.w]
        ps, cs = idx
        p0 = ps.start or 0
        p1 = self.p if ps.stop is None else ps.stop
        c0 = cs.start or 0
        c1 = self.w if cs.stop is None else cs.stop
        return self.h[p0:p1, self.off + c0:self.off + c1]


def build(T=4096, upto=None, osz=None):
    OWN = T // 2
    NT = T // 128
    CAP = T // 2
    SC512 = T // 512
    OSZ = osz or min(256, OWN)
    NSP = OWN // OSZ
    TS = min(512, OSZ)
    NTS = OSZ // TS

    nc = bass.Bass("TRN2", target_bir_lowering=False)

    d_in = {}

    def din(name, shape, dt=F32):
        d_in[name] = nc.dram_tensor(name, list(shape), dt, kind="ExternalInput")
        return d_in[name]

    xT = din("xT", (D, T))
    din("Wdkv", (128, DC * LAT))
    din("Wdq", (128, DC * LAT))
    din("Wuq", (128, DC * LAT))
    din("Wuk", (128, DC * LAT))
    din("Wuv", (128, DC * LAT))
    din("Wdqmy", (128, 512))
    din("Wuqtmy", (128, 512))
    Wotmy = din("Wotmy", (128, DC * 512))
    din("WG", (128, DC * E))
    din("BGB", (128, E))
    W1 = din("W1", (E, 128, DC * FP))
    din("B1C", (128, 2 * E))
    din("B2", (E, D))
    din("GAM", (128, DC))
    din("BET", (128, DC))
    din("IDEN", (128, 128))
    din("TRIL", (128, 128))
    IOTA = din("IOTA", (16, T))
    din("TOFF", (1, 1), I32)
    W2 = din("W2", (E, 128, 2 * D))
    B2P = din("B2P", (32, 512))

    out_fm = nc.dram_tensor("out_fm", [D, OWN], F32, kind="ExternalOutput")

    y_src = nc.dram_tensor("y_src", [512, T], F32)
    y_g2 = [nc.dram_tensor(f"y_g2_{k}", [256, T], F32) for k in range(4)]
    x1_d = nc.dram_tensor("x1_d", [D, T], F32)
    m_d = nc.dram_tensor("m_d", [16, T], F32)
    mown_d = nc.dram_tensor("mown_d", [16, T // 2], F32)

    P = Prog(nc)
    P.marks = {}

    def mark(name):
        P.marks[name] = {e: len(P.q[e]) for e in ENGS}
    from contextlib import ExitStack
    ctx = ExitStack()
    dA = P.sem("dA")
    dG = P.sem("dG")
    sT = P.sem("sT")
    sV = P.sem("sV")
    sS = P.sem("sS")
    sC = P.sem("sC")
    dU0 = P.sem("dU0")
    dU1 = P.sem("dU1")
    sP = P.sem("sP")
    dW = P.sem("dW")

    def sb(name, shape, dt=F32):
        return ctx.enter_context(nc.sbuf_tensor(name, list(shape), dt))

    def psb(name):
        return ctx.enter_context(nc.psum_tensor(name, [128, 512], F32))

    wdkv_sb = sb("wdkv", (128, DC * LAT))
    wdq_sb = sb("wdq", (128, DC * LAT))
    wuv_sb = sb("wuv", (128, DC * LAT))
    wdqmy_sb = sb("wdqmy", (128, 512))
    wuqtmy_sb = sb("wuqtmy", (128, 512))
    wg_sb = sb("wg", (128, DC * E))
    bgb_sb = sb("bgb", (128, E))
    b1c_sb = sb("b1c", (128, 2 * E))
    b2_sb = sb("b2", (16, D))
    gam_sb = sb("gam", (128, DC))
    bet_sb = sb("bet", (128, DC))
    iden_sb = sb("iden", (128, 128))
    tril_sb = sb("tril", (128, 128))
    toff_sb = sb("toff", (1, 1), I32)
    ones1_sb = sb("ones1", (1, 128))
    onesc_sb = sb("onesc", (128, 1))
    ke1_sb = sb("ke1", (128, 128))
    keff_sb = sb("keff", (128, 512))
    ve_sb = sb("ve", (128, 512))
    kefft_sb = sb("kefft", (128, 512))
    ctx_sb = sb("ctxb", (128, 128))
    sm_sb = sb("sm", (128, 16))
    row_sb = sb("row", (1, 1024))
    mub_sb = sb("mub", (128, 512))
    rsb_sb = sb("rsb", (128, 512))
    sq_sb = sb("sq", (128, 512))
    sq2_sb = sb("sq2", (128, 512))
    lgt_sb = sb("lgt", (128, E))
    mtk_sb = sb("mtk", (128, E))
    tq_sb = sb("tq", (128, 512))
    eps_sb = sb("eps", (1, 1))

    off_ckvT = 0
    off_ckv = T
    off_q1T = 2 * T
    off_KC = 3 * T
    off_qTh = 4 * T
    off_probsT = 5 * T
    off_yTh = 6 * T
    W_SC = max(T, DC * 512)
    off_xc = 7 * T
    off_s1 = 7 * T + W_SC
    off_s2 = 7 * T + 2 * W_SC
    off_mrows = 0
    off_cumsum = T
    off_zeros = 2 * T
    off_iota = 3 * T
    off_x1own = 0
    off_outs = off_x1own + 2 * DC * OSZ
    off_h = off_outs + 2 * DC * OSZ
    off_w1 = off_h + E * 2 * OSZ
    off_w2 = off_w1 + 2 * DC * FP
    off_mb = off_w2 + 2 * 2 * D
    off_mt = off_mb + 2 * OSZ
    off_b2p = off_mt + 2 * OSZ
    AR = max(7 * T + 3 * W_SC, off_b2p + 512, 4 * T)

    arena = sb("arena", (128, AR))
    ckvT_sb = Seg(arena, off_ckvT, 128, T)
    ckv_sb = Seg(arena, off_ckv, 128, T)
    q1T_sb = Seg(arena, off_q1T, 128, T)
    KC_sb = Seg(arena, off_KC, 128, T)
    qTh_sb = Seg(arena, off_qTh, 128, T)
    probsT_sb = Seg(arena, off_probsT, 128, T)
    yTh_sb = Seg(arena, off_yTh, 128, T)
    xc_sb = Seg(arena, off_xc, 128, DC * 512)
    sc3 = [Seg(arena, off_xc, 128, W_SC), Seg(arena, off_s1, 128, W_SC),
           Seg(arena, off_s2, 128, W_SC)]
    CDB = T >= DC * 512
    if CDB:
        xcC = [Seg(arena, off_xc, 128, DC * 512), Seg(arena, off_KC, 128, DC * 512)]
        scC = [Seg(arena, off_s1, 128, DC * 512), Seg(arena, off_qTh, 128, DC * 512)]
    else:
        xcC = [Seg(arena, off_xc, 128, DC * 512)] * 2
        scC = [Seg(arena, off_s1, 128, DC * 512)] * 2
    mrows_sb = Seg(arena, off_mrows, 16, T)
    cums_sb = Seg(arena, off_cumsum, 16, T)
    zers_sb = Seg(arena, off_zeros, 16, T)
    iota_sb = Seg(arena, off_iota, 16, T)
    x1own2 = [Seg(arena, off_x1own + i * DC * OSZ, 128, DC * OSZ) for i in range(2)]
    outs2 = [Seg(arena, off_outs + i * DC * OSZ, 128, DC * OSZ) for i in range(2)]
    h_sb = Seg(arena, off_h, 128, E * 2 * OSZ)
    w1_sb = [Seg(arena, off_w1 + i * DC * FP, 128, DC * FP) for i in range(2)]
    w2_sb = [Seg(arena, off_w2 + i * 2 * D, 128, 2 * D) for i in range(2)]
    mb_sb = [Seg(arena, off_mb + i * OSZ, 128, OSZ) for i in range(2)]
    mt_sb = Seg(arena, off_mt, 16, OSZ)
    mtz_sb = Seg(arena, off_mt, 32, 2 * OSZ)
    b2p_sb = Seg(arena, off_b2p, 32, 512)

    ps0, ps1 = psb("ps0"), psb("ps1")
    ps2, ps3 = psb("ps2"), psb("ps3")
    ps4, ps5 = psb("ps4"), psb("ps5")
    ps6, ps7 = psb("ps6"), psb("ps7")

    def dma(eng_key, out, in_, sem=None):
        sem = sem or (dA if eng_key == "sync" else dG)
        P.dma_barrier(eng_key, sem)
        hh = P.h(sem)
        P.emit(eng_key, lambda e, o=out, i=in_, hh=hh: e.dma_start(out=o, in_=i).then_inc(hh, 16))
        return P.bump(sem, 16)

    def tinc(eng_key, build_fn, sem):
        hh = P.h(sem)
        P.emit(eng_key, lambda e, f=build_fn, hh=hh: f(e).then_inc(hh, 1))
        return P.bump(sem, 1)

    def raw(eng_key, build_fn):
        P.emit(eng_key, build_fn)

    def vself(v):
        raw("vector", lambda e, v=v: e.wait_ge(P.h(sV), v))

    STATICS = [("Wdkv", wdkv_sb), ("Wdq", wdq_sb), ("Wuv", wuv_sb),
               ("Wdqmy", wdqmy_sb),
               ("Wuqtmy", wuqtmy_sb), ("WG", wg_sb), ("BGB", bgb_sb),
               ("B1C", b1c_sb), ("B2", b2_sb), ("GAM", gam_sb),
               ("BET", bet_sb), ("IDEN", iden_sb), ("TRIL", tril_sb),
               ("TOFF", toff_sb)]
    for name, dst in STATICS:
        sh = d_in[name].shape
        dma("sync", dst[0:sh[0], 0:sh[1]], d_in[name][0:sh[0], 0:sh[1]])
    ld0 = P.cnt[dA]
    P.wait("vector", dA, ld0)
    raw("vector", lambda e: e.memset(ones1_sb[:], 1.0))
    raw("vector", lambda e: e.memset(eps_sb[:], EPS))
    v_const = tinc("vector", lambda e: e.memset(onesc_sb[:], 1.0), sV)

    REG = {}

    def _init_regs(e):
        REG["toff"] = e.alloc_register("r_toff")
        REG["tmp"] = e.alloc_register("r_tmp")
        e.wait_ge(P.h(dA), ld0)
        e.reg_load(REG["toff"], toff_sb[:1, :1])
    P.emit("gpsimd", _init_regs)

    mark("loads")
    P.wait("tensor", dA, ld0)
    P.wait("tensor", sV, v_const)

    dma("sync", xc_sb[0:128, 0:DC * LAT], d_in["Wuq"][0:128, 0:DC * LAT])
    dma("sync", xc_sb[0:128, DC * LAT:2 * DC * LAT], d_in["Wuk"][0:128, 0:DC * LAT])
    P.wait("tensor", dA, P.cnt[dA])
    for dc in range(DC):
        fn = lambda e, dc=dc: e.matmul(
            ps5[:, 0:128], xc_sb[0:128, dc * LAT:(dc + 1) * LAT],
            xc_sb[0:128, DC * LAT + dc * LAT: DC * LAT + (dc + 1) * LAT],
            start=(dc == 0), stop=(dc == DC - 1))
        if dc < DC - 1:
            raw("tensor", fn)
        else:
            t = tinc("tensor", fn, sT)
    P.wait("vector", sT, t)
    v_ke1 = tinc("vector", lambda e: e.tensor_copy(ke1_sb[:], ps5[:, 0:128]), sV)

    P.wait("tensor", sV, v_ke1)
    for j in range(4):
        t = tinc("tensor", lambda e, j=j: e.matmul(
            ps5[:, 128:256], wdqmy_sb[:, j * 128:(j + 1) * 128], ke1_sb[:],
            start=True, stop=True), sT)
        P.wait("vector", sT, t)
        v = tinc("vector", lambda e, j=j: e.tensor_copy(
            keff_sb[:, j * 128:(j + 1) * 128], ps5[:, 128:256]), sV)
        P.wait("tensor", sV, v)
    for j in range(4):
        t = tinc("tensor", lambda e, j=j: e.transpose(
            ps6[:, 0:128], keff_sb[:, j * 128:(j + 1) * 128], iden_sb[:]), sT)
        P.wait("vector", sT, t)
        v = tinc("vector", lambda e, j=j: e.tensor_copy(
            kefft_sb[:, j * 128:(j + 1) * 128], ps6[:, 0:128]), sV)
        P.wait("tensor", sV, v)

    P.wait("sync", sT, P.cnt[sT])
    for dc in range(DC):
        dma("sync", xc_sb[0:128, dc * 512:(dc + 1) * 512],
            Wotmy[0:128, dc * 512:(dc + 1) * 512])
    P.wait("tensor", dA, P.cnt[dA])
    for dc in range(DC):
        fn = lambda e, dc=dc: e.matmul(
            ps5[:, 0:512], wuv_sb[:, dc * LAT:(dc + 1) * LAT],
            xc_sb[0:128, dc * 512:(dc + 1) * 512], start=(dc == 0), stop=(dc == DC - 1))
        if dc < DC - 1:
            raw("tensor", fn)
        else:
            t = tinc("tensor", fn, sT)
    P.wait("vector", sT, t)
    v_ve = tinc("vector", lambda e: e.tensor_copy(ve_sb[:], ps5[:, 0:512]), sV)

    for sc in range(SC512):
        P.wait("sync", sT, P.cnt[sT])
        for dc in range(DC):
            dma("sync", xc_sb[0:128, dc * 512:(dc + 1) * 512],
                xT[dc * 128:(dc + 1) * 128, sc * 512:(sc + 1) * 512])
        P.wait("tensor", dA, P.cnt[dA])
        if sc == 0:
            P.wait("tensor", sV, v_ve)
        for dc in range(DC):
            fn = lambda e, dc=dc: e.matmul(
                ps5[:, 0:512], wdkv_sb[:, dc * LAT:(dc + 1) * LAT],
                xc_sb[0:128, dc * 512:(dc + 1) * 512], start=(dc == 0), stop=(dc == DC - 1))
            if dc < DC - 1:
                raw("tensor", fn)
            else:
                t = tinc("tensor", fn, sT)
        for dc in range(DC):
            fn = lambda e, dc=dc: e.matmul(
                ps4[:, 0:512], wdq_sb[:, dc * LAT:(dc + 1) * LAT],
                xc_sb[0:128, dc * 512:(dc + 1) * 512], start=(dc == 0), stop=(dc == DC - 1))
            if dc < DC - 1:
                raw("tensor", fn)
            else:
                t2 = tinc("tensor", fn, sT)
        P.wait("vector", sT, t)
        v = tinc("vector", lambda e, sc=sc: e.tensor_copy(
            ckvT_sb[0:128, sc * 512:(sc + 1) * 512], ps5[:, 0:512]), sV)
        P.wait("vector", sT, t2)
        v2 = tinc("vector", lambda e, sc=sc: e.tensor_copy(
            q1T_sb[0:128, sc * 512:(sc + 1) * 512], ps4[:, 0:512]), sV)
        P.wait("tensor", sV, v2)
        bank = ps6 if sc % 2 == 0 else ps7
        P.wait("tensor", sS, P.cnt[sS])
        for j in range(4):
            fn = lambda e, sc=sc, j=j, bank=bank: e.transpose(
                bank[:, j * 128:(j + 1) * 128],
                ckvT_sb[0:128, sc * 512 + j * 128: sc * 512 + (j + 1) * 128], iden_sb[:])
            if j < 3:
                raw("tensor", fn)
            else:
                t = tinc("tensor", fn, sT)
        P.wait("scalar", sT, t)
        tinc("scalar", lambda e, sc=sc, bank=bank: e.copy(
            ckv_sb[0:128, sc * 512:(sc + 1) * 512], bank[:, 0:512]), sS)

    mark("prologue")
    g_up = [0, 0]
    dU = [dU0, dU1]
    SBANKS = [ps0, ps1, ps2, ps5]
    BG_s = {id(b): 0 for b in SBANKS}
    BG_v = {id(b): 0 for b in SBANKS}

    def emit_head_prologue(h):
        po = 64 * (h % 2)
        for kind in range(2):
            for sc in range(SC512):
                bank = ps5 if sc % 2 == 0 else ps2
                P.wait("tensor", sS, BG_s[id(bank)])
                P.wait("tensor", sV, BG_v[id(bank)])
                if kind == 0:
                    t = tinc("tensor", lambda e, sc=sc, po=po, h=h, bank=bank: e.matmul(
                        bank[po:po + 64, 0:512], kefft_sb[:, h * 64:(h + 1) * 64],
                        ckvT_sb[0:128, sc * 512:(sc + 1) * 512],
                        start=True, stop=True, tile_position=(0, po)), sT)
                    dstf = lambda e, sc=sc, po=po, bank=bank: e.copy(
                        KC_sb[po:po + 64, sc * 512:(sc + 1) * 512], bank[po:po + 64, 0:512])
                else:
                    t = tinc("tensor", lambda e, sc=sc, po=po, h=h, bank=bank: e.matmul(
                        bank[po:po + 64, 0:512], wuqtmy_sb[:, h * 64:(h + 1) * 64],
                        q1T_sb[0:128, sc * 512:(sc + 1) * 512],
                        start=True, stop=True, tile_position=(0, po)), sT)
                    dstf = lambda e, sc=sc, po=po, bank=bank: e.copy(
                        qTh_sb[po:po + 64, sc * 512:(sc + 1) * 512], bank[po:po + 64, 0:512])
                P.wait("scalar", sT, t)
                BG_s[id(bank)] = tinc("scalar", dstf, sS)

    def emit_S(h, g, i):
        po = 64 * (h % 2)
        smo = 4 * (i % 3)
        buf = sc3[i % 3]
        L = 128 * (g + 1)
        nch = (L + 511) // 512
        if g == 0:
            P.wait("tensor", sS, P.cnt[sS])
        ch_ev = {}
        t = 0
        for sch in range(nch):
            w = min(512, L - sch * 512)
            bank = SBANKS[sch % 4]
            P.wait("tensor", sS, BG_s[id(bank)])
            P.wait("tensor", sV, BG_v[id(bank)])
            t = tinc("tensor", lambda e, sch=sch, w=w, bank=bank, po=po, g=g: e.matmul(
                bank[:, 0:w], qTh_sb[po:po + 64, g * 128:(g + 1) * 128],
                KC_sb[po:po + 64, sch * 512: sch * 512 + w],
                start=True, stop=True), sT)
            is_last = (sch == nch - 1)
            fw = w - 128 if is_last else w
            if fw > 0:
                P.wait("scalar", sT, t)
                ch_ev[sch] = tinc("scalar", lambda e, sch=sch, fw=fw, bank=bank, buf=buf: e.activation(
                    buf[0:128, sch * 512: sch * 512 + fw], bank[:, 0:fw],
                    AF.Copy, bias=0.0, scale=0.125), sS)
                BG_s[id(bank)] = ch_ev[sch]
                P.wait("vector", sS, ch_ev[sch])
                tcol = smo + 3
                v = tinc("vector", lambda e, sch=sch, fw=fw, buf=buf, tcol=tcol: e.reduce_max(
                    sm_sb[:, tcol:tcol + 1], buf[0:128, sch * 512: sch * 512 + fw],
                    axis=AX.X, negate=True), sV)
                vself(v)
                if sch == 0:
                    v = tinc("vector", lambda e, smo=smo, tcol=tcol: e.tensor_copy(
                        sm_sb[:, smo:smo + 1], sm_sb[:, tcol:tcol + 1]), sV)
                else:
                    v = tinc("vector", lambda e, smo=smo, tcol=tcol: e.tensor_tensor(
                        sm_sb[:, smo:smo + 1], sm_sb[:, smo:smo + 1],
                        sm_sb[:, tcol:tcol + 1], ALU.min), sV)
                vself(v)
            else:
                ch_ev[sch] = P.cnt[sS]
        s_ev = P.cnt[sS]
        P.wait("vector", sT, t)
        P.wait("vector", sS, s_ev)
        lbank = SBANKS[(nch - 1) % 4]
        v_stt = tinc("vector", lambda e, L=L, buf=buf, bank=lbank: e.scalar_tensor_tensor(
            buf[0:128, L - 128: L], bank[:, (min(512, L - (nch - 1) * 512)) - 128: min(512, L - (nch - 1) * 512)],
            0.125, tril_sb[:], op0=ALU.mult, op1=ALU.add), sV)
        BG_v[id(lbank)] = v_stt
        vself(v_stt)
        tcol = smo + 3
        v = tinc("vector", lambda e, L=L, buf=buf, tcol=tcol: e.reduce_max(
            sm_sb[:, tcol:tcol + 1], buf[0:128, L - 128: L], axis=AX.X, negate=True), sV)
        vself(v)
        if L > 128:
            v_max = tinc("vector", lambda e, smo=smo, tcol=tcol: e.tensor_tensor(
                sm_sb[:, smo:smo + 1], sm_sb[:, smo:smo + 1],
                sm_sb[:, tcol:tcol + 1], ALU.min), sV)
        else:
            v_max = tinc("vector", lambda e, smo=smo, tcol=tcol: e.tensor_copy(
                sm_sb[:, smo:smo + 1], sm_sb[:, tcol:tcol + 1]), sV)
        vself(v_max)
        return dict(h=h, g=g, po=po, smo=smo, buf=buf, L=L, s_ev=s_ev,
                    v_stt=v_stt, v_max=v_max)

    def emit_S_tail(st):
        P.wait("scalar", sV, st["v_max"])
        P.wait("scalar", sS, P.cnt[sS])
        s_exp = tinc("scalar", lambda e, st=st: e.activation(
            st["buf"][0:128, 0:st["L"]], st["buf"][0:128, 0:st["L"]], AF.Exp,
            bias=sm_sb[:, st["smo"]:st["smo"] + 1], scale=1.0,
            accum_out=sm_sb[:, st["smo"] + 1:st["smo"] + 2]), sS)
        P.wait("vector", sS, s_exp)
        v = tinc("vector", lambda e, st=st: e.reciprocal(
            sm_sb[:, st["smo"] + 2:st["smo"] + 3], sm_sb[:, st["smo"] + 1:st["smo"] + 2]), sV)
        vself(v)
        st["v_mult"] = tinc("vector", lambda e, st=st: e.tensor_scalar_mul(
            st["buf"][0:128, 0:st["L"]], st["buf"][0:128, 0:st["L"]],
            sm_sb[:, st["smo"] + 2:st["smo"] + 3]), sV)

    def emit_C(st):
        h, g, po, buf = st["h"], st["g"], st["po"], st["buf"]
        nb = g + 1
        P.wait("tensor", sV, st["v_mult"])
        pT = {}
        for j0 in range(0, nb, 4):
            jn = min(4, nb - j0)
            bank = ps6 if (j0 // 4) % 2 == 0 else ps7
            if j0 >= 8:
                P.wait("tensor", sS, pT[j0 - 8])
            for j in range(j0, j0 + jn):
                fn = lambda e, j=j, j0=j0, bank=bank, buf=buf: e.transpose(
                    bank[:, (j - j0) * 128:(j - j0 + 1) * 128],
                    buf[0:128, j * 128:(j + 1) * 128], iden_sb[:])
                if j < j0 + jn - 1:
                    raw("tensor", fn)
                else:
                    t = tinc("tensor", fn, sT)
            P.wait("scalar", sT, t)
            pT[j0] = tinc("scalar", lambda e, j0=j0, jn=jn, bank=bank: e.copy(
                probsT_sb[0:128, j0 * 128:(j0 + jn) * 128], bank[:, 0:jn * 128]), sS)
        P.wait("tensor", sS, pT[(nb - 1) // 4 * 4])
        for j in range(nb):
            fn = lambda e, j=j, nb=nb: e.matmul(
                ps3[:, 0:128], ckv_sb[0:128, j * 128:(j + 1) * 128],
                probsT_sb[0:128, j * 128:(j + 1) * 128],
                start=(j == 0), stop=(j == nb - 1))
            if j < nb - 1:
                raw("tensor", fn)
            else:
                t = tinc("tensor", fn, sT)
        P.wait("scalar", sT, t)
        s_cx = tinc("scalar", lambda e: e.copy(ctx_sb[:], ps3[:, 0:128]), sS)
        P.wait("tensor", sS, s_cx)
        t = tinc("tensor", lambda e, h=h, po=po: e.matmul(
            ps4[po:po + 64, 0:128], ve_sb[:, h * 64:(h + 1) * 64], ctx_sb[:],
            start=True, stop=True, tile_position=(0, po)), sT)
        P.wait("vector", sT, t)
        if g == 0:
            P.wait("vector", dU[h % 2], g_up[h % 2])
        st["v_y"] = tinc("vector", lambda e, h=h, g=g, po=po: e.tensor_copy(
            yTh_sb[po:po + 64, g * 128:(g + 1) * 128], ps4[po:po + 64, 0:128]), sV)

    def upload_head(hp):
        P.wait("gpsimd", sV, P.cnt[sV])
        g_up[hp % 2] = dma(
            "gpsimd", y_src[64 * hp:64 * hp + 64, 0:T],
            yTh_sb[64 * (hp % 2):64 * (hp % 2) + 64, 0:T], sem=dU[hp % 2])
        if hp % 2 == 1:
            k = hp // 2
            P.wait("gpsimd", dU0, P.cnt[dU0])
            P.wait("gpsimd", dU1, P.cnt[dU1])
            tinc("gpsimd", lambda e, k=k: e.collective_compute(
                "AllGather", ALU.bypass,
                replica_groups=[[0, 1], [2, 3], [4, 5], [6, 7]],
                ins=[y_src[128 * k:128 * k + 128, 0:T].opt()],
                outs=[y_g2[k][0:256, 0:T].opt()]), sC)

    STATE = {"prev": None}
    steps = [(h, g) for h in range(HPC) for g in range(NT)]
    hist = []
    for i, (h, g) in enumerate(steps):
        if g == 0 and h == 0:
            emit_head_prologue(0)
        if g == NT - 2 and h + 1 < HPC:
            emit_head_prologue(h + 1)
        st = emit_S(h, g, i)
        STATE["prev"] = st
        hist.append(st)
        if i >= 2:
            pv = hist[i - 2]
            emit_C(pv)
            if pv["g"] == NT - 1:
                upload_head(pv["h"])
        if i >= 1:
            emit_S_tail(hist[i - 1])
    pv = hist[-2]
    emit_C(pv)
    if pv["g"] == NT - 1:
        upload_head(pv["h"])
    emit_S_tail(hist[-1])
    pv = hist[-1]
    emit_C(pv)
    if pv["g"] == NT - 1:
        upload_head(pv["h"])

    mark("attention")
    mark("exchange")
    P.wait("sync", sC, P.cnt[sC])
    P.wait("sync", sV, P.cnt[sV])
    P.wait("sync", sT, P.cnt[sT])
    GPD = 5
    cons_gate = [None, None]
    ld_val = [0] * (SC512 + 1)

    def c_loads(sc):
        b = (sc % 2) if CDB else 0
        if cons_gate[b] is not None:
            P.wait("sync", sV, cons_gate[b]["v"])
            P.wait("sync", sT, cons_gate[b]["t"])
            P.wait("sync", sP, cons_gate[b]["p"])
            P.wait("sync", dA, cons_gate[b]["a"])
        for dc in range(DC):
            dma("sync", xcC[b][0:128, dc * 512:(dc + 1) * 512],
                xT[dc * 128:(dc + 1) * 128, sc * 512:(sc + 1) * 512])
        for dc in range(DC):
            ysrc = y_g2[dc] if dc < 4 else y_g2[dc - 4]
            ro = 0 if dc < 4 else 128
            dma("sync", scC[b][0:128, dc * 512:(dc + 1) * 512],
                ysrc[ro:ro + 128, sc * 512:(sc + 1) * 512])
        ld_val[sc] = P.cnt[dA]
        P.pending[dA] = max(P.pending[dA], P.cnt[dA])

    def c_compute(sc):
        b = (sc % 2) if CDB else 0
        xb, sb_ = xcC[b], scC[b]
        P.wait("vector", dA, ld_val[sc])
        P.wait("gpsimd", dA, ld_val[sc])
        for dc in range(GPD):
            fn = lambda e, dc=dc: e.tensor_add(
                xb[0:128, dc * 512:(dc + 1) * 512],
                xb[0:128, dc * 512:(dc + 1) * 512],
                sb_[0:128, dc * 512:(dc + 1) * 512])
            if dc < GPD - 1:
                raw("vector", fn)
            else:
                v_z = tinc("vector", fn, sV)
        for dc in range(GPD, DC):
            fn = lambda e, dc=dc: e.tensor_add(
                xb[0:128, dc * 512:(dc + 1) * 512],
                xb[0:128, dc * 512:(dc + 1) * 512],
                sb_[0:128, dc * 512:(dc + 1) * 512])
            if dc < DC - 1:
                raw("gpsimd", fn)
            else:
                p_z = tinc("gpsimd", fn, sP)
        P.wait("scalar", sV, v_z)
        P.wait("scalar", sP, p_z)
        P.wait("tensor", sV, v_z)
        P.wait("tensor", sP, p_z)
        for dc in range(DC):
            fn = lambda e, dc=dc: e.matmul(
                ps4[0:1, 0:512], onesc_sb[:], xb[0:128, dc * 512:(dc + 1) * 512],
                start=(dc == 0), stop=(dc == DC - 1))
            if dc < DC - 1:
                raw("tensor", fn)
            else:
                tinc("tensor", fn, sT)
        sq_mm = {}
        for dc in range(DC):
            sqb = sq_sb if dc % 2 == 0 else sq2_sb
            if dc >= 2:
                P.wait("scalar", sT, sq_mm[dc - 2])
            s = tinc("scalar", lambda e, dc=dc, sqb=sqb: e.activation(
                sqb[:], xb[0:128, dc * 512:(dc + 1) * 512], AF.Square,
                bias=0.0, scale=1.0), sS)
            P.wait("tensor", sS, s)
            sq_mm[dc] = tinc("tensor", lambda e, dc=dc, sqb=sqb: e.matmul(
                ps5[0:1, 0:512], onesc_sb[:], sqb[:],
                start=(dc == 0), stop=(dc == DC - 1)), sT)
        P.wait("vector", sT, P.cnt[sT])
        raw("vector", lambda e: e.tensor_scalar_mul(row_sb[:, 0:512], ps4[0:1, 0:512], 1.0 / D))
        v = tinc("vector", lambda e: e.tensor_scalar_mul(row_sb[:, 512:1024], ps5[0:1, 0:512], 1.0 / D), sV)
        vself(v)
        v = tinc("vector", lambda e: e.tensor_tensor(
            sq_sb[0:1, 0:512], row_sb[:, 0:512], row_sb[:, 0:512], ALU.mult), sV)
        vself(v)
        v = tinc("vector", lambda e: e.tensor_sub(
            row_sb[:, 512:1024], row_sb[:, 512:1024], sq_sb[0:1, 0:512]), sV)
        P.wait("scalar", sV, v)
        s = tinc("scalar", lambda e: e.activation(
            row_sb[:, 512:1024], row_sb[:, 512:1024], AF.Sqrt, bias=eps_sb[0:1, 0:1], scale=1.0), sS)
        P.wait("vector", sS, s)
        v_rs = tinc("vector", lambda e: e.reciprocal(row_sb[:, 512:1024], row_sb[:, 512:1024]), sV)
        P.wait("tensor", sV, v_rs)
        raw("tensor", lambda e: e.matmul(ps4[:, 0:512], ones1_sb[:], row_sb[:, 0:512],
                                         start=True, stop=True))
        t = tinc("tensor", lambda e: e.matmul(ps5[:, 0:512], ones1_sb[:], row_sb[:, 512:1024],
                                              start=True, stop=True), sT)
        P.wait("vector", sT, t)
        raw("vector", lambda e: e.tensor_copy(mub_sb[:], ps4[:, 0:512]))
        v_bc = tinc("vector", lambda e: e.tensor_copy(rsb_sb[:], ps5[:, 0:512]), sV)
        vself(v_bc)
        for dc in range(GPD):
            v = tinc("vector", lambda e, dc=dc: e.tensor_sub(
                sb_[0:128, dc * 512:(dc + 1) * 512],
                xb[0:128, dc * 512:(dc + 1) * 512], mub_sb[:]), sV)
            vself(v)
            v_x1 = tinc("vector", lambda e, dc=dc: e.tensor_mul(
                sb_[0:128, dc * 512:(dc + 1) * 512],
                sb_[0:128, dc * 512:(dc + 1) * 512], rsb_sb[:]), sV)
        P.wait("gpsimd", sV, v_bc)
        for dc in range(GPD, DC):
            p = tinc("gpsimd", lambda e, dc=dc: e.tensor_sub(
                sb_[0:128, dc * 512:(dc + 1) * 512],
                xb[0:128, dc * 512:(dc + 1) * 512], mub_sb[:]), sP)
            raw("gpsimd", lambda e, p=p: e.wait_ge(P.h(sP), p))
            p_x1 = tinc("gpsimd", lambda e, dc=dc: e.tensor_mul(
                sb_[0:128, dc * 512:(dc + 1) * 512],
                sb_[0:128, dc * 512:(dc + 1) * 512], rsb_sb[:]), sP)
        P.wait("sync", sV, v_x1)
        P.wait("sync", sP, p_x1)
        for dc in range(DC):
            dma("sync", x1_d[dc * 128:(dc + 1) * 128, sc * 512:(sc + 1) * 512],
                sb_[0:128, dc * 512:(dc + 1) * 512])
        st_val = P.cnt[dA]
        P.pending[dA] = max(P.pending[dA], st_val)
        P.wait("tensor", sV, v_x1)
        P.wait("tensor", sP, p_x1)
        for tt in range(4):
            for dc in range(DC):
                fn = lambda e, tt=tt, dc=dc: e.matmul(
                    ps4[:, 0:E],
                    sb_[0:128, dc * 512 + tt * 128: dc * 512 + tt * 128 + 128],
                    wg_sb[:, dc * E:(dc + 1) * E], start=(dc == 0), stop=(dc == DC - 1))
                if dc < DC - 1:
                    raw("tensor", fn)
                else:
                    t = tinc("tensor", fn, sT)
            P.wait("vector", sT, t)
            v = tinc("vector", lambda e: e.tensor_add(lgt_sb[:], ps4[:, 0:E], bgb_sb[:]), sV)
            vself(v)
            v = tinc("vector", lambda e: e.max(out=sm_sb[:, 0:8], in_=lgt_sb[:]), sV)
            vself(v)
            v = tinc("vector", lambda e: e.tensor_copy(sm_sb[:, 8:9], sm_sb[:, 1:2]), sV)
            vself(v)
            v_m = tinc("vector", lambda e: e.tensor_scalar(
                mtk_sb[:], lgt_sb[:], sm_sb[:, 8:9], None, op0=ALU.is_ge), sV)
            P.wait("tensor", sV, v_m)
            t = tinc("tensor", lambda e: e.transpose(ps4[0:16, 0:128], mtk_sb[:], iden_sb[:]), sT)
            P.wait("vector", sT, t)
            v = tinc("vector", lambda e, sc=sc, tt=tt: e.tensor_copy(
                mrows_sb[0:16, sc * 512 + tt * 128: sc * 512 + tt * 128 + 128],
                ps4[0:16, 0:128]), sV)
            P.wait("tensor", sV, v)
        cons_gate[b] = {"v": P.cnt[sV], "t": P.cnt[sT], "p": P.cnt[sP], "a": st_val}

    if CDB:
        c_loads(0)
        for sc in range(SC512):
            if sc + 1 < SC512:
                c_loads(sc + 1)
            c_compute(sc)
    else:
        for sc in range(SC512):
            c_loads(sc)
            c_compute(sc)

    mark("lnc")
    P.wait("sync", dG, P.cnt[dG])
    dma("sync", iota_sb[:], IOTA[0:16, 0:T])
    P.wait("vector", dA, P.cnt[dA])
    v = tinc("vector", lambda e: e.memset(zers_sb[:], 0.0), sV)
    vself(v)
    v = tinc("vector", lambda e: e.reduce_sum(sm_sb[0:16, 0:1], mrows_sb[:], axis=AX.X), sV)
    vself(v)
    v = tinc("vector", lambda e: e.tensor_scalar(
        sm_sb[0:16, 1:2], sm_sb[0:16, 0:1], -1.0, float(CAP), op0=ALU.mult, op1=ALU.add), sV)
    vself(v)
    v = tinc("vector", lambda e: e.tensor_tensor_scan(
        cums_sb[:], mrows_sb[:], zers_sb[:], 0.0, op0=ALU.add, op1=ALU.add), sV)
    vself(v)
    v = tinc("vector", lambda e: e.tensor_sub(cums_sb[:], iota_sb[:], cums_sb[:]), sV)
    vself(v)
    v = tinc("vector", lambda e: e.tensor_scalar(
        cums_sb[:], cums_sb[:], sm_sb[0:16, 1:2], None, op0=ALU.is_lt), sV)
    vself(v)
    v = tinc("vector", lambda e: e.tensor_scalar(
        zers_sb[:], mrows_sb[:], -1.0, 1.0, op0=ALU.mult, op1=ALU.add), sV)
    vself(v)
    v = tinc("vector", lambda e: e.tensor_tensor(cums_sb[:], cums_sb[:], zers_sb[:], ALU.min), sV)
    vself(v)
    v_fin = tinc("vector", lambda e: e.tensor_tensor(mrows_sb[:], mrows_sb[:], cums_sb[:], ALU.max), sV)

    P.wait("gpsimd", sV, v_fin)
    dma("gpsimd", m_d[0:16, 0:T], mrows_sb[:])
    P.wait("gpsimd", dG, P.cnt[dG])
    P.wait("gpsimd", dA, P.cnt[dA])

    mark("fill")
    used_gate = [0, 0]
    x1_gate = [None, None]
    out_gate = [0, 0]
    mt_gate = 0
    ln_prev = None

    def obank(dc):
        return [ps0, ps1, ps2, ps3][dc // 2]

    def ooff(dc):
        return (dc % 2) * OSZ

    def pull_own(k):
        par = k % 2
        P.wait("gpsimd", sT, max(mt_gate, P.cnt[sT] if k > 0 else 0))
        if k > 0:
            P.wait("gpsimd", sV, P.cnt[sV])
            P.wait("gpsimd", sS, P.cnt[sS])
        q0 = k * OSZ

        def _pull(e, q0=q0, par=par, gv=P.cnt[dG] + 16):
            e.reg_alu(REG["tmp"], REG["toff"], q0, ALU.add)
            e.dma_start(out=mt_sb[:],
                        in_=bass.AP(m_d, REG["tmp"], [[T, 16], [1, OSZ]])).then_inc(P.h(dG), 16)
            e.wait_ge(P.h(dG), gv)
            e.dma_start(out=bass.AP(arena, off_x1own + par * DC * OSZ,
                                    [[AR, 128], [OSZ, DC], [1, OSZ]]),
                        in_=bass.AP(x1_d, REG["tmp"], [[T, 128], [128 * T, DC], [1, OSZ]])
                        ).then_inc(P.h(dG), 16)
            e.dma_start(out=mown_d[0:16, 0:OSZ], in_=mt_sb[:]).then_inc(P.h(dG), 16)
            e.dma_start(out=mtz_sb[16:32, OSZ:2 * OSZ],
                        in_=mt_sb[:]).then_inc(P.h(dG), 16)
        P.dma_barrier("gpsimd", dG)
        P.emit("gpsimd", _pull)
        P.bump(dG, 16 * 4)
        P.pending[dG] = max(P.pending[dG], P.cnt[dG])
        return P.cnt[dG]

    def prefetch_w(e_idx, buf):
        dma("gpsimd", w1_sb[buf][:], W1[e_idx, 0:128, 0:DC * FP])
        dma("sync", w2_sb[buf][:], W2[e_idx, 0:128, 0:2 * D], sem=dW)

    def prefetch_mb(e_idx, buf, gown):
        P.wait("gpsimd", dG, gown)

        def _mb(e, e_idx=e_idx, buf=buf):
            e.dma_start(out=mb_sb[buf][:],
                        in_=bass.AP(mown_d, e_idx * (T // 2), [[0, 128], [1, OSZ]])
                        ).then_inc(P.h(dG), 16)
        P.dma_barrier("gpsimd", dG)
        P.emit("gpsimd", _mb)
        P.bump(dG, 16)

    def emit_ln2(src_sb, dst_sb, affine, resid_sb, v_ready, s_ready, bank_a, bank_b):
        P.wait("scalar", sV, v_ready)
        P.wait("tensor", sV, v_ready)
        if s_ready:
            P.wait("tensor", sS, s_ready)
        for dc in range(DC):
            fn = lambda e, dc=dc, bank_a=bank_a: e.matmul(
                bank_a[0:1, 0:OSZ], onesc_sb[:],
                src_sb[0:128, dc * OSZ:(dc + 1) * OSZ],
                start=(dc == 0), stop=(dc == DC - 1))
            if dc < DC - 1:
                raw("tensor", fn)
            else:
                tinc("tensor", fn, sT)
        sq_mm = {}
        for dc in range(DC):
            sqb = sq_sb if dc % 2 == 0 else sq2_sb
            if dc >= 2:
                P.wait("scalar", sT, sq_mm[dc - 2])
            s = tinc("scalar", lambda e, dc=dc, sqb=sqb: e.activation(
                sqb[:, 0:OSZ], src_sb[0:128, dc * OSZ:(dc + 1) * OSZ],
                AF.Square, bias=0.0, scale=1.0), sS)
            P.wait("tensor", sS, s)
            sq_mm[dc] = tinc("tensor", lambda e, dc=dc, sqb=sqb, bank_b=bank_b: e.matmul(
                bank_b[0:1, 0:OSZ], onesc_sb[:], sqb[:, 0:OSZ],
                start=(dc == 0), stop=(dc == DC - 1)), sT)
        P.wait("vector", sT, P.cnt[sT])
        raw("vector", lambda e, bank_a=bank_a: e.tensor_scalar_mul(
            row_sb[:, 0:OSZ], bank_a[0:1, 0:OSZ], 1.0 / D))
        v = tinc("vector", lambda e, bank_b=bank_b: e.tensor_scalar_mul(
            row_sb[:, 512:512 + OSZ], bank_b[0:1, 0:OSZ], 1.0 / D), sV)
        vself(v)
        v = tinc("vector", lambda e: e.tensor_tensor(
            sq_sb[0:1, 0:OSZ], row_sb[:, 0:OSZ], row_sb[:, 0:OSZ], ALU.mult), sV)
        vself(v)
        v = tinc("vector", lambda e: e.tensor_sub(
            row_sb[:, 512:512 + OSZ], row_sb[:, 512:512 + OSZ], sq_sb[0:1, 0:OSZ]), sV)
        P.wait("scalar", sV, v)
        s = tinc("scalar", lambda e: e.activation(
            row_sb[:, 512:512 + OSZ], row_sb[:, 512:512 + OSZ], AF.Sqrt,
            bias=eps_sb[0:1, 0:1], scale=1.0), sS)
        P.wait("vector", sS, s)
        v_rs = tinc("vector", lambda e: e.reciprocal(row_sb[:, 512:512 + OSZ], row_sb[:, 512:512 + OSZ]), sV)
        P.wait("tensor", sV, v_rs)
        raw("tensor", lambda e, bank_a=bank_a: e.matmul(
            bank_a[:, 256:256 + OSZ], ones1_sb[:], row_sb[:, 0:OSZ],
            start=True, stop=True))
        t = tinc("tensor", lambda e, bank_b=bank_b: e.matmul(
            bank_b[:, 256:256 + OSZ], ones1_sb[:], row_sb[:, 512:512 + OSZ],
            start=True, stop=True), sT)
        P.wait("vector", sT, t)
        raw("vector", lambda e, bank_a=bank_a: e.tensor_copy(mub_sb[:, 0:OSZ], bank_a[:, 256:256 + OSZ]))
        v = tinc("vector", lambda e, bank_b=bank_b: e.tensor_copy(rsb_sb[:, 0:OSZ], bank_b[:, 256:256 + OSZ]), sV)
        vself(v)
        vv = 0
        for dc in range(DC):
            v = tinc("vector", lambda e, dc=dc: e.tensor_sub(
                tq_sb[:, 0:OSZ], src_sb[0:128, dc * OSZ:(dc + 1) * OSZ],
                mub_sb[:, 0:OSZ]), sV)
            vself(v)
            v = tinc("vector", lambda e: e.tensor_mul(
                tq_sb[:, 0:OSZ], tq_sb[:, 0:OSZ], rsb_sb[:, 0:OSZ]), sV)
            vself(v)
            if affine:
                v = tinc("vector", lambda e, dc=dc: e.tensor_scalar(
                    tq_sb[:, 0:OSZ], tq_sb[:, 0:OSZ], gam_sb[:, dc:dc + 1],
                    bet_sb[:, dc:dc + 1], op0=ALU.mult, op1=ALU.add), sV)
                vself(v)
            if resid_sb is not None:
                fn = lambda e, dc=dc: e.tensor_add(
                    dst_sb[0:128, dc * OSZ:(dc + 1) * OSZ], tq_sb[:, 0:OSZ],
                    resid_sb[0:128, dc * OSZ:(dc + 1) * OSZ])
            else:
                fn = lambda e, dc=dc: e.tensor_copy(
                    dst_sb[0:128, dc * OSZ:(dc + 1) * OSZ], tq_sb[:, 0:OSZ])
            vv = tinc("vector", fn, sV)
            vself(vv)
        return vv

    LNG = {"v": 0}

    def emit_ln_tail(k):
        par = k % 2
        x1b, ob = x1own2[par], outs2[par]
        v_rdy = max(zev[par], LNG["v"])
        v_z3 = emit_ln2(ob, x1b, True, x1b, v_rdy, 0, ps6, ps7)
        v_out = emit_ln2(x1b, ob, False, None, v_z3, 0, ps6, ps7)
        P.wait("sync", sV, v_out)
        for dc in range(DC):
            dma("sync", out_fm[dc * 128:(dc + 1) * 128, k * OSZ:(k + 1) * OSZ],
                ob[0:128, dc * OSZ:(dc + 1) * OSZ])
        out_gate[par] = P.cnt[dA]
        P.pending[dA] = max(P.pending[dA], out_gate[par])
        LNG["v"] = v_out

    zev = [0, 0]
    for _eng in ("sync", "gpsimd", "vector"):
        for _sem in (sT, sV, sS, sP, dA, dG):
            P.wait(_eng, _sem, P.cnt[_sem])
    dma("gpsimd", b2p_sb[0:32, 0:512], B2P[0:32, 0:512])
    v_mtz = tinc("vector", lambda e: e.memset(mtz_sb[0:32, 0:2 * OSZ], 0.0), sV)
    P.wait("gpsimd", sV, v_mtz)
    g_own = pull_own(0)
    prefetch_w(0, 0)
    prefetch_mb(0, 0, g_own)
    wr1 = [P.cnt[dG], 0]
    wr2 = [P.cnt[dW], 0]
    for k in range(NSP):
        par = k % 2
        x1b, ob = x1own2[par], outs2[par]
        P.wait("tensor", sV, max(zev[(k - 1) % 2] if k >= 1 else 0, v_mtz))
        P.wait("tensor", dG, g_own)
        for d in range(4):
            fn = lambda e, d=d: e.matmul(
                [ps0, ps1, ps2, ps3][d][:, 0:2 * OSZ],
                b2p_sb[0:32, d * 128:(d + 1) * 128],
                mtz_sb[0:32, 0:2 * OSZ],
                start=True, stop=False, skip_group_check=True)
            if d < 3:
                raw("tensor", fn)
            else:
                t_b2 = tinc("tensor", fn, sT)
        mt_gate = t_b2
        gel_gate = {}
        for ei in range(E):
            buf = ei % 2
            P.wait("tensor", dG, wr1[buf])
            P.wait("tensor", dW, wr2[buf])
            if ei + 1 < E:
                nbuf = (ei + 1) % 2
                if used_gate[nbuf]:
                    P.wait("gpsimd", sT, used_gate[nbuf])
                    P.wait("sync", sT, used_gate[nbuf])
                prefetch_w(ei + 1, nbuf)
                prefetch_mb(ei + 1, nbuf, g_own)
                wr1[nbuf] = P.cnt[dG]
                wr2[nbuf] = P.cnt[dW]
            for fc in range(2):
                kk = ei * 2 + fc
                bank = ps4 if kk % 2 == 0 else ps5
                P.wait("tensor", sS, gel_gate.get(kk - 2, 0))
                for dc in range(DC):
                    fn = lambda e, buf=buf, fc=fc, dc=dc, bank=bank, x1b=x1b: e.matmul(
                        bank[:, 0:OSZ],
                        w1_sb[buf][0:128, dc * FP + fc * 128: dc * FP + fc * 128 + 128],
                        x1b[0:128, dc * OSZ:(dc + 1) * OSZ],
                        start=(dc == 0), stop=(dc == DC - 1))
                    if dc < DC - 1:
                        raw("tensor", fn)
                    else:
                        t = tinc("tensor", fn, sT)
                P.wait("scalar", sT, t)
                s = tinc("scalar", lambda e, fc=fc, ei=ei, bank=bank: e.activation(
                    h_sb[0:128, (ei * 2 + fc) * OSZ:(ei * 2 + fc + 1) * OSZ], bank[:, 0:OSZ],
                    AF.Gelu, bias=b1c_sb[:, 2 * ei + fc: 2 * ei + fc + 1], scale=1.0), sS)
                gel_gate[kk] = s
                P.wait("vector", sS, s)
                P.wait("vector", dG, wr1[buf])
                fn = lambda e, buf=buf, ei=ei, fc=fc: e.tensor_mul(
                    h_sb[0:128, (ei * 2 + fc) * OSZ:(ei * 2 + fc + 1) * OSZ],
                    h_sb[0:128, (ei * 2 + fc) * OSZ:(ei * 2 + fc + 1) * OSZ],
                    mb_sb[buf][0:128, 0:OSZ])
                if fc == 1:
                    v_h = tinc("vector", fn, sV)
                else:
                    raw("vector", fn)
            P.wait("tensor", sV, v_h)
            for dc in range(DC):
                for fc in range(2):
                    fn = lambda e, buf=buf, dc=dc, fc=fc, ei=ei: e.matmul(
                        [ps0, ps1, ps2, ps3][dc // 2][:, (dc % 2) * OSZ:(dc % 2 + 1) * OSZ],
                        w2_sb[buf][0:128, fc * D + dc * 128: fc * D + dc * 128 + 128],
                        h_sb[0:128, (ei * 2 + fc) * OSZ:(ei * 2 + fc + 1) * OSZ],
                        start=False, stop=(ei == E - 1 and fc == 1 and dc % 2 == 1),
                        skip_group_check=True)
                    if dc == DC - 1 and fc == 1:
                        t = tinc("tensor", fn, sT)
                    else:
                        raw("tensor", fn)
            used_gate[buf] = P.cnt[sT]
        P.wait("vector", sT, P.cnt[sT])
        if out_gate[par]:
            P.wait("vector", dA, out_gate[par])
        for dc in range(DC):
            fn = lambda e, dc=dc, ob=ob, x1b=x1b: e.tensor_add(
                ob[0:128, dc * OSZ:(dc + 1) * OSZ],
                [ps0, ps1, ps2, ps3][dc // 2][:, (dc % 2) * OSZ:(dc % 2 + 1) * OSZ],
                x1b[0:128, dc * OSZ:(dc + 1) * OSZ])
            if dc < DC - 1:
                raw("vector", fn)
            else:
                zev[par] = tinc("vector", fn, sV)
        vself(zev[par])
        mark(f"sp{k}")
        if ln_prev is not None:
            emit_ln_tail(ln_prev)
        ln_prev = k
        if k + 1 < NSP:
            g_own = pull_own(k + 1)
            if used_gate[0]:
                P.wait("gpsimd", sT, used_gate[0])
                P.wait("sync", sT, used_gate[0])
            prefetch_w(0, 0)
            prefetch_mb(0, 0, g_own)
            wr1[0] = P.cnt[dG]
            wr2[0] = P.cnt[dW]
    emit_ln_tail(ln_prev)

    P.wait("sync", dA, P.cnt[dA])
    mark("all")

    if upto is not None:
        cut = P.marks[upto]
        for e in ENGS:
            P.q[e] = P.q[e][:cut[e]]

    with ctx:
        with nc.Block() as block:
            @block.sync
            def _(e):
                for f in P.q["sync"]:
                    f(e)

            @block.gpsimd
            def _(e):
                for f in P.q["gpsimd"]:
                    f(e)

            @block.tensor
            def _(e):
                for f in P.q["tensor"]:
                    f(e)

            @block.vector
            def _(e):
                for f in P.q["vector"]:
                    f(e)

            @block.scalar
            def _(e):
                for f in P.q["scalar"]:
                    f(e)

    return nc


def prep_inputs(inputs, T=4096):
    x = np.asarray(inputs["L0_attn_x"], np.float32)
    W_dq = np.asarray(inputs["L0_attn_W_dq"], np.float32)
    W_uq = np.asarray(inputs["L0_attn_W_uq"], np.float32)
    W_dkv = np.asarray(inputs["L0_attn_W_dkv"], np.float32)
    W_uk = np.asarray(inputs["L0_attn_W_uk"], np.float32)
    W_uv = np.asarray(inputs["L0_attn_W_uv"], np.float32)
    W_o = np.asarray(inputs["L0_attn_W_o"], np.float32)
    wg = np.asarray(inputs["L0_ffn_wg"], np.float32)
    bg = np.asarray(inputs["L0_ffn_bg"], np.float32)
    w1 = np.asarray(inputs["L0_ffn_w1"], np.float32)
    b1 = np.asarray(inputs["L0_ffn_b1"], np.float32)
    w2 = np.asarray(inputs["L0_ffn_w2"], np.float32)
    b2 = np.asarray(inputs["L0_ffn_b2"], np.float32)
    gamma = np.asarray(inputs["L0_ffn_gamma"], np.float32)
    beta = np.asarray(inputs["L0_ffn_beta"], np.float32)

    def cm(a):
        K = a.shape[1]
        return np.ascontiguousarray(a.reshape(DC, 128, K).transpose(1, 0, 2).reshape(128, DC * K))

    shared = {
        "Wdkv": cm(W_dkv.T), "Wdq": cm(W_dq.T), "Wuq": cm(W_uq),
        "Wuk": cm(W_uk), "Wuv": cm(W_uv), "WG": cm(wg),
        "BGB": np.ascontiguousarray(np.tile(bg[None, :], (128, 1))),
        "W1": np.ascontiguousarray(w1.reshape(E, DC, 128, FP).transpose(0, 2, 1, 3).reshape(E, 128, DC * FP)),
        "B1C": np.ascontiguousarray(b1.reshape(E, 2, 128).transpose(2, 0, 1).reshape(128, 2 * E)),
        "W2": np.ascontiguousarray(w2.reshape(E, 2, 128, D).transpose(0, 2, 1, 3).reshape(E, 128, 2 * D)),
        "B2": np.ascontiguousarray(b2),
        "B2P": np.ascontiguousarray(np.concatenate(
            [b2.reshape(E, DC // 2, 2, 128)[:, :, 0, :].reshape(E, 512),
             b2.reshape(E, DC // 2, 2, 128)[:, :, 1, :].reshape(E, 512)], axis=0)),
        "GAM": np.ascontiguousarray(gamma.reshape(DC, 128).T),
        "BET": np.ascontiguousarray(beta.reshape(DC, 128).T),
        "IDEN": np.eye(128, dtype=np.float32),
        "TRIL": np.where(np.arange(128)[None, :] <= np.arange(128)[:, None],
                         0.0, NEG).astype(np.float32),
        "IOTA": np.ascontiguousarray(np.tile(np.arange(T, dtype=np.float32)[None, :], (16, 1))),
    }
    WoT = W_o.T
    WuqT = W_uq.T
    in_maps = []
    for i in range(8):
        b, p = i // 2, i % 2
        m = dict(shared)
        m["xT"] = np.ascontiguousarray(x[b].T)
        m["Wdqmy"] = np.ascontiguousarray(W_dq[:, 512 * p:512 * p + 512])
        m["Wuqtmy"] = np.ascontiguousarray(WuqT[:, 512 * p:512 * p + 512])
        m["Wotmy"] = cm(WoT[:, 512 * p:512 * p + 512])
        m["TOFF"] = np.array([[T // 2 * p]], np.int32)
        in_maps.append(m)
    return in_maps


def kernel(**inputs):
    T = int(np.asarray(inputs["L0_attn_x"]).shape[1])
    in_maps = prep_inputs(inputs, T)
    nc = build(T)
    res = run_bass_kernel_spmd(nc, in_maps, core_ids=list(range(8)))
    B = int(np.asarray(inputs["L0_attn_x"]).shape[0])
    out = np.empty((B, T, D), np.float32)
    OWN = T // 2
    for i in range(8):
        b, p = i // 2, i % 2
        out[b, OWN * p: OWN * (p + 1), :] = np.asarray(res.results[i]["out_fm"]).T
    return out


# version 11
# speedup vs baseline: 1.1092x; 1.1046x over previous
import numpy as np
import concourse.bass as bass
import concourse.mybir as mybir
from concourse.bass_utils import run_bass_kernel_spmd

F32 = mybir.dt.float32
I32 = mybir.dt.int32
AF = mybir.ActivationFunctionType
ALU = mybir.AluOpType
AX = mybir.AxisListType

D = 1024
LAT = 128
HPC = 8
E = 16
FP = 256
DC = D // 128
EPS = 1e-5
NEG = -1e30

ENGS = ("sync", "gpsimd", "tensor", "vector", "scalar")
SKIP = set()


class Prog:

    def __init__(self, nc):
        self.nc = nc
        self.q = {e: [] for e in ENGS}
        self.cnt = {}
        self.sems = {}
        self.pending = {}
        self.barrier = {}

    def sem(self, name):
        h = self.nc.semaphore(name).__enter__()
        self.sems[name] = h
        self.cnt[name] = 0
        self.pending[name] = 0
        self.barrier[name] = 0
        return name

    def emit(self, eng, fn):
        self.q[eng].append(fn)

    def wait(self, eng, sem, val=None):
        v = self.cnt[sem] if val is None else val
        h = self.sems[sem]
        if v > 0:
            self.emit(eng, lambda e, h=h, v=v: e.wait_ge(h, v))
            self.pending[sem] = max(self.pending[sem], v)
        return v

    def dma_barrier(self, eng_key, sem):
        if self.pending[sem] > self.barrier[sem]:
            v = self.pending[sem]
            h = self.sems[sem]
            self.emit(eng_key, lambda e, h=h, v=v: e.wait_ge(h, v))
            self.barrier[sem] = v

    def bump(self, sem, n):
        self.cnt[sem] += n
        return self.cnt[sem]

    def h(self, sem):
        return self.sems[sem]


class Seg:

    def __init__(self, h, off, p, w):
        self.h, self.off, self.p, self.w = h, off, p, w

    def __getitem__(self, idx):
        if idx == slice(None):
            return self.h[0:self.p, self.off:self.off + self.w]
        ps, cs = idx
        p0 = ps.start or 0
        p1 = self.p if ps.stop is None else ps.stop
        c0 = cs.start or 0
        c1 = self.w if cs.stop is None else cs.stop
        return self.h[p0:p1, self.off + c0:self.off + c1]


def build(T=4096, upto=None, osz=None):
    OWN = T // 2
    NT = T // 128
    CAP = T // 2
    SC512 = T // 512
    OSZ = osz or min(1024, OWN)
    NSP = OWN // OSZ
    TS = min(512, OSZ)
    NTS = OSZ // TS

    nc = bass.Bass("TRN2", target_bir_lowering=False)

    d_in = {}

    def din(name, shape, dt=F32):
        d_in[name] = nc.dram_tensor(name, list(shape), dt, kind="ExternalInput")
        return d_in[name]

    xT = din("xT", (D, T))
    din("Wdkv", (128, DC * LAT))
    din("Wdq", (128, DC * LAT))
    din("Wuq", (128, DC * LAT))
    din("Wuk", (128, DC * LAT))
    din("Wuv", (128, DC * LAT))
    din("Wdqmy", (128, 512))
    din("Wuqtmy", (128, 512))
    Wotmy = din("Wotmy", (128, DC * 512))
    din("WG", (128, DC * E))
    din("BGB", (128, E))
    W1 = din("W1", (E, 128, DC * FP))
    din("B1C", (128, 2 * E))
    din("B2", (E, D))
    din("GAM", (128, DC))
    din("BET", (128, DC))
    din("IDEN", (128, 128))
    din("TRIL", (128, 128))
    IOTA = din("IOTA", (16, T))
    din("TOFF", (1, 1), I32)
    W2 = din("W2", (E, 128, 2 * D))

    out_fm = nc.dram_tensor("out_fm", [D, OWN], F32, kind="ExternalOutput")

    y_src = nc.dram_tensor("y_src", [512, T], F32)
    y_g2 = [nc.dram_tensor(f"y_g2_{k}", [256, T], F32) for k in range(4)]
    x1_d = nc.dram_tensor("x1_d", [D, T], F32)
    m_d = nc.dram_tensor("m_d", [16, T], F32)
    mown_d = nc.dram_tensor("mown_d", [16, T // 2], F32)

    P = Prog(nc)
    P.marks = {}

    def mark(name):
        P.marks[name] = {e: len(P.q[e]) for e in ENGS}
    from contextlib import ExitStack
    ctx = ExitStack()
    dA = P.sem("dA")
    dG = P.sem("dG")
    sT = P.sem("sT")
    sV = P.sem("sV")
    sS = P.sem("sS")
    sC = P.sem("sC")
    dU0 = P.sem("dU0")
    dU1 = P.sem("dU1")
    sP = P.sem("sP")

    def sb(name, shape, dt=F32):
        return ctx.enter_context(nc.sbuf_tensor(name, list(shape), dt))

    def psb(name):
        return ctx.enter_context(nc.psum_tensor(name, [128, 512], F32))

    wdkv_sb = sb("wdkv", (128, DC * LAT))
    wdq_sb = sb("wdq", (128, DC * LAT))
    wuv_sb = sb("wuv", (128, DC * LAT))
    wdqmy_sb = sb("wdqmy", (128, 512))
    wuqtmy_sb = sb("wuqtmy", (128, 512))
    wg_sb = sb("wg", (128, DC * E))
    bgb_sb = sb("bgb", (128, E))
    b1c_sb = sb("b1c", (128, 2 * E))
    b2_sb = sb("b2", (16, D))
    gam_sb = sb("gam", (128, DC))
    bet_sb = sb("bet", (128, DC))
    iden_sb = sb("iden", (128, 128))
    tril_sb = sb("tril", (128, 128))
    toff_sb = sb("toff", (1, 1), I32)
    ones1_sb = sb("ones1", (1, 128))
    onesc_sb = sb("onesc", (128, 1))
    ke1_sb = sb("ke1", (128, 128))
    keff_sb = sb("keff", (128, 512))
    ve_sb = sb("ve", (128, 512))
    kefft_sb = sb("kefft", (128, 512))
    ctx_sb = sb("ctxb", (128, 128))
    sm_sb = sb("sm", (128, 16))
    row_sb = sb("row", (1, 1024))
    mub_sb = sb("mub", (128, 512))
    rsb_sb = sb("rsb", (128, 512))
    sq_sb = sb("sq", (128, 512))
    sq2_sb = sb("sq2", (128, 512))
    lgt_sb = sb("lgt", (128, E))
    mtk_sb = sb("mtk", (128, E))
    tq_sb = sb("tq", (128, 512))
    eps_sb = sb("eps", (1, 1))

    off_ckvT = 0
    off_ckv = T
    off_q1T = 2 * T
    off_KC = 3 * T
    off_qTh = 4 * T
    off_probsT = 5 * T
    off_yTh = 6 * T
    W_SC = max(T, DC * 512)
    off_xc = 7 * T
    off_s1 = 7 * T + W_SC
    off_s2 = 7 * T + 2 * W_SC
    off_mrows = 0
    off_cumsum = T
    off_zeros = 2 * T
    off_iota = 3 * T
    off_x1own = 0
    off_outs = off_x1own + DC * OSZ
    off_h = off_outs + DC * OSZ
    off_w1 = off_h + 2 * OSZ
    off_w2 = off_w1 + 2 * DC * FP
    off_mb = off_w2 + 2 * 2 * D
    off_mt = off_mb + 2 * OSZ
    AR = max(7 * T + 3 * W_SC, off_mt + OSZ, 4 * T)

    arena = sb("arena", (128, AR))
    ckvT_sb = Seg(arena, off_ckvT, 128, T)
    ckv_sb = Seg(arena, off_ckv, 128, T)
    q1T_sb = Seg(arena, off_q1T, 128, T)
    KC_sb = Seg(arena, off_KC, 128, T)
    qTh_sb = Seg(arena, off_qTh, 128, T)
    probsT_sb = Seg(arena, off_probsT, 128, T)
    yTh_sb = Seg(arena, off_yTh, 128, T)
    xc_sb = Seg(arena, off_xc, 128, DC * 512)
    sc3 = [Seg(arena, off_xc, 128, W_SC), Seg(arena, off_s1, 128, W_SC),
           Seg(arena, off_s2, 128, W_SC)]
    CDB = T >= DC * 512
    if CDB:
        xcC = [Seg(arena, off_xc, 128, DC * 512), Seg(arena, off_KC, 128, DC * 512)]
        scC = [Seg(arena, off_s1, 128, DC * 512), Seg(arena, off_qTh, 128, DC * 512)]
    else:
        xcC = [Seg(arena, off_xc, 128, DC * 512)] * 2
        scC = [Seg(arena, off_s1, 128, DC * 512)] * 2
    mrows_sb = Seg(arena, off_mrows, 16, T)
    cums_sb = Seg(arena, off_cumsum, 16, T)
    zers_sb = Seg(arena, off_zeros, 16, T)
    iota_sb = Seg(arena, off_iota, 16, T)
    x1own_sb = Seg(arena, off_x1own, 128, DC * OSZ)
    outs_sb = Seg(arena, off_outs, 128, DC * OSZ)
    h_sb = Seg(arena, off_h, 128, 2 * OSZ)
    w1_sb = [Seg(arena, off_w1 + i * DC * FP, 128, DC * FP) for i in range(2)]
    w2_sb = [Seg(arena, off_w2 + i * 2 * D, 128, 2 * D) for i in range(2)]
    mb_sb = [Seg(arena, off_mb + i * OSZ, 128, OSZ) for i in range(2)]
    mt_sb = Seg(arena, off_mt, 16, OSZ)

    ps0, ps1 = psb("ps0"), psb("ps1")
    ps2, ps3 = psb("ps2"), psb("ps3")
    ps4, ps5 = psb("ps4"), psb("ps5")
    ps6, ps7 = psb("ps6"), psb("ps7")

    def dma(eng_key, out, in_, sem=None):
        sem = sem or (dA if eng_key == "sync" else dG)
        P.dma_barrier(eng_key, sem)
        hh = P.h(sem)
        P.emit(eng_key, lambda e, o=out, i=in_, hh=hh: e.dma_start(out=o, in_=i).then_inc(hh, 16))
        return P.bump(sem, 16)

    def tinc(eng_key, build_fn, sem):
        hh = P.h(sem)
        P.emit(eng_key, lambda e, f=build_fn, hh=hh: f(e).then_inc(hh, 1))
        return P.bump(sem, 1)

    def raw(eng_key, build_fn):
        P.emit(eng_key, build_fn)

    def vself(v):
        raw("vector", lambda e, v=v: e.wait_ge(P.h(sV), v))

    STATICS = [("Wdkv", wdkv_sb), ("Wdq", wdq_sb), ("Wuv", wuv_sb),
               ("Wdqmy", wdqmy_sb),
               ("Wuqtmy", wuqtmy_sb), ("WG", wg_sb), ("BGB", bgb_sb),
               ("B1C", b1c_sb), ("B2", b2_sb), ("GAM", gam_sb),
               ("BET", bet_sb), ("IDEN", iden_sb), ("TRIL", tril_sb),
               ("TOFF", toff_sb)]
    for name, dst in STATICS:
        sh = d_in[name].shape
        dma("sync", dst[0:sh[0], 0:sh[1]], d_in[name][0:sh[0], 0:sh[1]])
    ld0 = P.cnt[dA]
    P.wait("vector", dA, ld0)
    raw("vector", lambda e: e.memset(ones1_sb[:], 1.0))
    raw("vector", lambda e: e.memset(eps_sb[:], EPS))
    v_const = tinc("vector", lambda e: e.memset(onesc_sb[:], 1.0), sV)

    REG = {}

    def _init_regs(e):
        REG["toff"] = e.alloc_register("r_toff")
        REG["tmp"] = e.alloc_register("r_tmp")
        e.wait_ge(P.h(dA), ld0)
        e.reg_load(REG["toff"], toff_sb[:1, :1])
    P.emit("gpsimd", _init_regs)

    mark("loads")
    P.wait("tensor", dA, ld0)
    P.wait("tensor", sV, v_const)

    dma("sync", xc_sb[0:128, 0:DC * LAT], d_in["Wuq"][0:128, 0:DC * LAT])
    dma("sync", xc_sb[0:128, DC * LAT:2 * DC * LAT], d_in["Wuk"][0:128, 0:DC * LAT])
    P.wait("tensor", dA, P.cnt[dA])
    for dc in range(DC):
        fn = lambda e, dc=dc: e.matmul(
            ps5[:, 0:128], xc_sb[0:128, dc * LAT:(dc + 1) * LAT],
            xc_sb[0:128, DC * LAT + dc * LAT: DC * LAT + (dc + 1) * LAT],
            start=(dc == 0), stop=(dc == DC - 1))
        if dc < DC - 1:
            raw("tensor", fn)
        else:
            t = tinc("tensor", fn, sT)
    P.wait("vector", sT, t)
    v_ke1 = tinc("vector", lambda e: e.tensor_copy(ke1_sb[:], ps5[:, 0:128]), sV)

    P.wait("tensor", sV, v_ke1)
    for j in range(4):
        t = tinc("tensor", lambda e, j=j: e.matmul(
            ps5[:, 128:256], wdqmy_sb[:, j * 128:(j + 1) * 128], ke1_sb[:],
            start=True, stop=True), sT)
        P.wait("vector", sT, t)
        v = tinc("vector", lambda e, j=j: e.tensor_copy(
            keff_sb[:, j * 128:(j + 1) * 128], ps5[:, 128:256]), sV)
        P.wait("tensor", sV, v)
    for j in range(4):
        t = tinc("tensor", lambda e, j=j: e.transpose(
            ps6[:, 0:128], keff_sb[:, j * 128:(j + 1) * 128], iden_sb[:]), sT)
        P.wait("vector", sT, t)
        v = tinc("vector", lambda e, j=j: e.tensor_copy(
            kefft_sb[:, j * 128:(j + 1) * 128], ps6[:, 0:128]), sV)
        P.wait("tensor", sV, v)

    P.wait("sync", sT, P.cnt[sT])
    for dc in range(DC):
        dma("sync", xc_sb[0:128, dc * 512:(dc + 1) * 512],
            Wotmy[0:128, dc * 512:(dc + 1) * 512])
    P.wait("tensor", dA, P.cnt[dA])
    for dc in range(DC):
        fn = lambda e, dc=dc: e.matmul(
            ps5[:, 0:512], wuv_sb[:, dc * LAT:(dc + 1) * LAT],
            xc_sb[0:128, dc * 512:(dc + 1) * 512], start=(dc == 0), stop=(dc == DC - 1))
        if dc < DC - 1:
            raw("tensor", fn)
        else:
            t = tinc("tensor", fn, sT)
    P.wait("vector", sT, t)
    v_ve = tinc("vector", lambda e: e.tensor_copy(ve_sb[:], ps5[:, 0:512]), sV)

    for sc in range(SC512):
        P.wait("sync", sT, P.cnt[sT])
        for dc in range(DC):
            dma("sync", xc_sb[0:128, dc * 512:(dc + 1) * 512],
                xT[dc * 128:(dc + 1) * 128, sc * 512:(sc + 1) * 512])
        P.wait("tensor", dA, P.cnt[dA])
        if sc == 0:
            P.wait("tensor", sV, v_ve)
        for dc in range(DC):
            fn = lambda e, dc=dc: e.matmul(
                ps5[:, 0:512], wdkv_sb[:, dc * LAT:(dc + 1) * LAT],
                xc_sb[0:128, dc * 512:(dc + 1) * 512], start=(dc == 0), stop=(dc == DC - 1))
            if dc < DC - 1:
                raw("tensor", fn)
            else:
                t = tinc("tensor", fn, sT)
        for dc in range(DC):
            fn = lambda e, dc=dc: e.matmul(
                ps4[:, 0:512], wdq_sb[:, dc * LAT:(dc + 1) * LAT],
                xc_sb[0:128, dc * 512:(dc + 1) * 512], start=(dc == 0), stop=(dc == DC - 1))
            if dc < DC - 1:
                raw("tensor", fn)
            else:
                t2 = tinc("tensor", fn, sT)
        P.wait("vector", sT, t)
        v = tinc("vector", lambda e, sc=sc: e.tensor_copy(
            ckvT_sb[0:128, sc * 512:(sc + 1) * 512], ps5[:, 0:512]), sV)
        P.wait("vector", sT, t2)
        v2 = tinc("vector", lambda e, sc=sc: e.tensor_copy(
            q1T_sb[0:128, sc * 512:(sc + 1) * 512], ps4[:, 0:512]), sV)
        P.wait("tensor", sV, v2)
        bank = ps6 if sc % 2 == 0 else ps7
        P.wait("tensor", sS, P.cnt[sS])
        for j in range(4):
            fn = lambda e, sc=sc, j=j, bank=bank: e.transpose(
                bank[:, j * 128:(j + 1) * 128],
                ckvT_sb[0:128, sc * 512 + j * 128: sc * 512 + (j + 1) * 128], iden_sb[:])
            if j < 3:
                raw("tensor", fn)
            else:
                t = tinc("tensor", fn, sT)
        P.wait("scalar", sT, t)
        tinc("scalar", lambda e, sc=sc, bank=bank: e.copy(
            ckv_sb[0:128, sc * 512:(sc + 1) * 512], bank[:, 0:512]), sS)

    mark("prologue")
    g_up = [0, 0]
    dU = [dU0, dU1]
    SBANKS = [ps0, ps1, ps2, ps5]
    BG_s = {id(b): 0 for b in SBANKS}
    BG_v = {id(b): 0 for b in SBANKS}

    def emit_head_prologue(h):
        po = 64 * (h % 2)
        for kind in range(2):
            for sc in range(SC512):
                bank = ps5 if sc % 2 == 0 else ps2
                P.wait("tensor", sS, BG_s[id(bank)])
                P.wait("tensor", sV, BG_v[id(bank)])
                if kind == 0:
                    t = tinc("tensor", lambda e, sc=sc, po=po, h=h, bank=bank: e.matmul(
                        bank[po:po + 64, 0:512], kefft_sb[:, h * 64:(h + 1) * 64],
                        ckvT_sb[0:128, sc * 512:(sc + 1) * 512],
                        start=True, stop=True, tile_position=(0, po)), sT)
                    dstf = lambda e, sc=sc, po=po, bank=bank: e.copy(
                        KC_sb[po:po + 64, sc * 512:(sc + 1) * 512], bank[po:po + 64, 0:512])
                else:
                    t = tinc("tensor", lambda e, sc=sc, po=po, h=h, bank=bank: e.matmul(
                        bank[po:po + 64, 0:512], wuqtmy_sb[:, h * 64:(h + 1) * 64],
                        q1T_sb[0:128, sc * 512:(sc + 1) * 512],
                        start=True, stop=True, tile_position=(0, po)), sT)
                    dstf = lambda e, sc=sc, po=po, bank=bank: e.copy(
                        qTh_sb[po:po + 64, sc * 512:(sc + 1) * 512], bank[po:po + 64, 0:512])
                P.wait("scalar", sT, t)
                BG_s[id(bank)] = tinc("scalar", dstf, sS)

    def emit_S(h, g, i):
        po = 64 * (h % 2)
        smo = 4 * (i % 3)
        buf = sc3[i % 3]
        L = 128 * (g + 1)
        nch = (L + 511) // 512
        if g == 0:
            P.wait("tensor", sS, P.cnt[sS])
        ch_ev = {}
        t = 0
        for sch in range(nch):
            w = min(512, L - sch * 512)
            bank = SBANKS[sch % 4]
            P.wait("tensor", sS, BG_s[id(bank)])
            P.wait("tensor", sV, BG_v[id(bank)])
            t = tinc("tensor", lambda e, sch=sch, w=w, bank=bank, po=po, g=g: e.matmul(
                bank[:, 0:w], qTh_sb[po:po + 64, g * 128:(g + 1) * 128],
                KC_sb[po:po + 64, sch * 512: sch * 512 + w],
                start=True, stop=True), sT)
            is_last = (sch == nch - 1)
            fw = w - 128 if is_last else w
            if fw > 0:
                P.wait("scalar", sT, t)
                ch_ev[sch] = tinc("scalar", lambda e, sch=sch, fw=fw, bank=bank, buf=buf: e.activation(
                    buf[0:128, sch * 512: sch * 512 + fw], bank[:, 0:fw],
                    AF.Copy, bias=0.0, scale=0.125), sS)
                BG_s[id(bank)] = ch_ev[sch]
                P.wait("vector", sS, ch_ev[sch])
                tcol = smo + 3
                v = tinc("vector", lambda e, sch=sch, fw=fw, buf=buf, tcol=tcol: e.reduce_max(
                    sm_sb[:, tcol:tcol + 1], buf[0:128, sch * 512: sch * 512 + fw],
                    axis=AX.X, negate=True), sV)
                vself(v)
                if sch == 0:
                    v = tinc("vector", lambda e, smo=smo, tcol=tcol: e.tensor_copy(
                        sm_sb[:, smo:smo + 1], sm_sb[:, tcol:tcol + 1]), sV)
                else:
                    v = tinc("vector", lambda e, smo=smo, tcol=tcol: e.tensor_tensor(
                        sm_sb[:, smo:smo + 1], sm_sb[:, smo:smo + 1],
                        sm_sb[:, tcol:tcol + 1], ALU.min), sV)
                vself(v)
            else:
                ch_ev[sch] = P.cnt[sS]
        s_ev = P.cnt[sS]
        P.wait("vector", sT, t)
        P.wait("vector", sS, s_ev)
        lbank = SBANKS[(nch - 1) % 4]
        v_stt = tinc("vector", lambda e, L=L, buf=buf, bank=lbank: e.scalar_tensor_tensor(
            buf[0:128, L - 128: L], bank[:, (min(512, L - (nch - 1) * 512)) - 128: min(512, L - (nch - 1) * 512)],
            0.125, tril_sb[:], op0=ALU.mult, op1=ALU.add), sV)
        BG_v[id(lbank)] = v_stt
        vself(v_stt)
        tcol = smo + 3
        v = tinc("vector", lambda e, L=L, buf=buf, tcol=tcol: e.reduce_max(
            sm_sb[:, tcol:tcol + 1], buf[0:128, L - 128: L], axis=AX.X, negate=True), sV)
        vself(v)
        if L > 128:
            v_max = tinc("vector", lambda e, smo=smo, tcol=tcol: e.tensor_tensor(
                sm_sb[:, smo:smo + 1], sm_sb[:, smo:smo + 1],
                sm_sb[:, tcol:tcol + 1], ALU.min), sV)
        else:
            v_max = tinc("vector", lambda e, smo=smo, tcol=tcol: e.tensor_copy(
                sm_sb[:, smo:smo + 1], sm_sb[:, tcol:tcol + 1]), sV)
        vself(v_max)
        return dict(h=h, g=g, po=po, smo=smo, buf=buf, L=L, s_ev=s_ev,
                    v_stt=v_stt, v_max=v_max)

    def emit_S_tail(st):
        P.wait("scalar", sV, st["v_max"])
        P.wait("scalar", sS, P.cnt[sS])
        s_exp = tinc("scalar", lambda e, st=st: e.activation(
            st["buf"][0:128, 0:st["L"]], st["buf"][0:128, 0:st["L"]], AF.Exp,
            bias=sm_sb[:, st["smo"]:st["smo"] + 1], scale=1.0,
            accum_out=sm_sb[:, st["smo"] + 1:st["smo"] + 2]), sS)
        P.wait("vector", sS, s_exp)
        v = tinc("vector", lambda e, st=st: e.reciprocal(
            sm_sb[:, st["smo"] + 2:st["smo"] + 3], sm_sb[:, st["smo"] + 1:st["smo"] + 2]), sV)
        vself(v)
        st["v_mult"] = tinc("vector", lambda e, st=st: e.tensor_scalar_mul(
            st["buf"][0:128, 0:st["L"]], st["buf"][0:128, 0:st["L"]],
            sm_sb[:, st["smo"] + 2:st["smo"] + 3]), sV)

    def emit_C(st):
        h, g, po, buf = st["h"], st["g"], st["po"], st["buf"]
        nb = g + 1
        P.wait("tensor", sV, st["v_mult"])
        pT = {}
        for j0 in range(0, nb, 4):
            jn = min(4, nb - j0)
            bank = ps6 if (j0 // 4) % 2 == 0 else ps7
            if j0 >= 8:
                P.wait("tensor", sS, pT[j0 - 8])
            for j in range(j0, j0 + jn):
                fn = lambda e, j=j, j0=j0, bank=bank, buf=buf: e.transpose(
                    bank[:, (j - j0) * 128:(j - j0 + 1) * 128],
                    buf[0:128, j * 128:(j + 1) * 128], iden_sb[:])
                if j < j0 + jn - 1:
                    raw("tensor", fn)
                else:
                    t = tinc("tensor", fn, sT)
            P.wait("scalar", sT, t)
            pT[j0] = tinc("scalar", lambda e, j0=j0, jn=jn, bank=bank: e.copy(
                probsT_sb[0:128, j0 * 128:(j0 + jn) * 128], bank[:, 0:jn * 128]), sS)
        P.wait("tensor", sS, pT[(nb - 1) // 4 * 4])
        for j in range(nb):
            fn = lambda e, j=j, nb=nb: e.matmul(
                ps3[:, 0:128], ckv_sb[0:128, j * 128:(j + 1) * 128],
                probsT_sb[0:128, j * 128:(j + 1) * 128],
                start=(j == 0), stop=(j == nb - 1))
            if j < nb - 1:
                raw("tensor", fn)
            else:
                t = tinc("tensor", fn, sT)
        P.wait("scalar", sT, t)
        s_cx = tinc("scalar", lambda e: e.copy(ctx_sb[:], ps3[:, 0:128]), sS)
        P.wait("tensor", sS, s_cx)
        t = tinc("tensor", lambda e, h=h, po=po: e.matmul(
            ps4[po:po + 64, 0:128], ve_sb[:, h * 64:(h + 1) * 64], ctx_sb[:],
            start=True, stop=True, tile_position=(0, po)), sT)
        P.wait("vector", sT, t)
        if g == 0:
            P.wait("vector", dU[h % 2], g_up[h % 2])
        st["v_y"] = tinc("vector", lambda e, h=h, g=g, po=po: e.tensor_copy(
            yTh_sb[po:po + 64, g * 128:(g + 1) * 128], ps4[po:po + 64, 0:128]), sV)

    def upload_head(hp):
        P.wait("gpsimd", sV, P.cnt[sV])
        g_up[hp % 2] = dma(
            "gpsimd", y_src[64 * hp:64 * hp + 64, 0:T],
            yTh_sb[64 * (hp % 2):64 * (hp % 2) + 64, 0:T], sem=dU[hp % 2])
        if hp % 2 == 1:
            k = hp // 2
            P.wait("gpsimd", dU0, P.cnt[dU0])
            P.wait("gpsimd", dU1, P.cnt[dU1])
            tinc("gpsimd", lambda e, k=k: e.collective_compute(
                "AllGather", ALU.bypass,
                replica_groups=[[0, 1], [2, 3], [4, 5], [6, 7]],
                ins=[y_src[128 * k:128 * k + 128, 0:T].opt()],
                outs=[y_g2[k][0:256, 0:T].opt()]), sC)

    STATE = {"prev": None}
    steps = [(h, g) for h in range(HPC) for g in range(NT)]
    hist = []
    for i, (h, g) in enumerate(steps):
        if g == 0 and h == 0:
            emit_head_prologue(0)
        if g == NT - 2 and h + 1 < HPC:
            emit_head_prologue(h + 1)
        st = emit_S(h, g, i)
        STATE["prev"] = st
        hist.append(st)
        if i >= 2:
            pv = hist[i - 2]
            emit_C(pv)
            if pv["g"] == NT - 1:
                upload_head(pv["h"])
        if i >= 1:
            emit_S_tail(hist[i - 1])
    pv = hist[-2]
    emit_C(pv)
    if pv["g"] == NT - 1:
        upload_head(pv["h"])
    emit_S_tail(hist[-1])
    pv = hist[-1]
    emit_C(pv)
    if pv["g"] == NT - 1:
        upload_head(pv["h"])

    mark("attention")
    mark("exchange")
    P.wait("sync", sC, P.cnt[sC])
    P.wait("sync", sV, P.cnt[sV])
    P.wait("sync", sT, P.cnt[sT])
    GPD = 5
    cons_gate = [None, None]
    ld_val = [0] * (SC512 + 1)

    def c_loads(sc):
        b = (sc % 2) if CDB else 0
        if cons_gate[b] is not None:
            P.wait("sync", sV, cons_gate[b]["v"])
            P.wait("sync", sT, cons_gate[b]["t"])
            P.wait("sync", sP, cons_gate[b]["p"])
            P.wait("sync", dA, cons_gate[b]["a"])
        for dc in range(DC):
            dma("sync", xcC[b][0:128, dc * 512:(dc + 1) * 512],
                xT[dc * 128:(dc + 1) * 128, sc * 512:(sc + 1) * 512])
        for dc in range(DC):
            ysrc = y_g2[dc] if dc < 4 else y_g2[dc - 4]
            ro = 0 if dc < 4 else 128
            dma("sync", scC[b][0:128, dc * 512:(dc + 1) * 512],
                ysrc[ro:ro + 128, sc * 512:(sc + 1) * 512])
        ld_val[sc] = P.cnt[dA]
        P.pending[dA] = max(P.pending[dA], P.cnt[dA])

    def c_compute(sc):
        b = (sc % 2) if CDB else 0
        xb, sb_ = xcC[b], scC[b]
        P.wait("vector", dA, ld_val[sc])
        P.wait("gpsimd", dA, ld_val[sc])
        for dc in range(GPD):
            fn = lambda e, dc=dc: e.tensor_add(
                xb[0:128, dc * 512:(dc + 1) * 512],
                xb[0:128, dc * 512:(dc + 1) * 512],
                sb_[0:128, dc * 512:(dc + 1) * 512])
            if dc < GPD - 1:
                raw("vector", fn)
            else:
                v_z = tinc("vector", fn, sV)
        for dc in range(GPD, DC):
            fn = lambda e, dc=dc: e.tensor_add(
                xb[0:128, dc * 512:(dc + 1) * 512],
                xb[0:128, dc * 512:(dc + 1) * 512],
                sb_[0:128, dc * 512:(dc + 1) * 512])
            if dc < DC - 1:
                raw("gpsimd", fn)
            else:
                p_z = tinc("gpsimd", fn, sP)
        P.wait("scalar", sV, v_z)
        P.wait("scalar", sP, p_z)
        P.wait("tensor", sV, v_z)
        P.wait("tensor", sP, p_z)
        for dc in range(DC):
            fn = lambda e, dc=dc: e.matmul(
                ps4[0:1, 0:512], onesc_sb[:], xb[0:128, dc * 512:(dc + 1) * 512],
                start=(dc == 0), stop=(dc == DC - 1))
            if dc < DC - 1:
                raw("tensor", fn)
            else:
                tinc("tensor", fn, sT)
        sq_mm = {}
        for dc in range(DC):
            sqb = sq_sb if dc % 2 == 0 else sq2_sb
            if dc >= 2:
                P.wait("scalar", sT, sq_mm[dc - 2])
            s = tinc("scalar", lambda e, dc=dc, sqb=sqb: e.activation(
                sqb[:], xb[0:128, dc * 512:(dc + 1) * 512], AF.Square,
                bias=0.0, scale=1.0), sS)
            P.wait("tensor", sS, s)
            sq_mm[dc] = tinc("tensor", lambda e, dc=dc, sqb=sqb: e.matmul(
                ps5[0:1, 0:512], onesc_sb[:], sqb[:],
                start=(dc == 0), stop=(dc == DC - 1)), sT)
        P.wait("vector", sT, P.cnt[sT])
        raw("vector", lambda e: e.tensor_scalar_mul(row_sb[:, 0:512], ps4[0:1, 0:512], 1.0 / D))
        v = tinc("vector", lambda e: e.tensor_scalar_mul(row_sb[:, 512:1024], ps5[0:1, 0:512], 1.0 / D), sV)
        vself(v)
        v = tinc("vector", lambda e: e.tensor_tensor(
            sq_sb[0:1, 0:512], row_sb[:, 0:512], row_sb[:, 0:512], ALU.mult), sV)
        vself(v)
        v = tinc("vector", lambda e: e.tensor_sub(
            row_sb[:, 512:1024], row_sb[:, 512:1024], sq_sb[0:1, 0:512]), sV)
        P.wait("scalar", sV, v)
        s = tinc("scalar", lambda e: e.activation(
            row_sb[:, 512:1024], row_sb[:, 512:1024], AF.Sqrt, bias=eps_sb[0:1, 0:1], scale=1.0), sS)
        P.wait("vector", sS, s)
        v_rs = tinc("vector", lambda e: e.reciprocal(row_sb[:, 512:1024], row_sb[:, 512:1024]), sV)
        P.wait("tensor", sV, v_rs)
        raw("tensor", lambda e: e.matmul(ps4[:, 0:512], ones1_sb[:], row_sb[:, 0:512],
                                         start=True, stop=True))
        t = tinc("tensor", lambda e: e.matmul(ps5[:, 0:512], ones1_sb[:], row_sb[:, 512:1024],
                                              start=True, stop=True), sT)
        P.wait("vector", sT, t)
        raw("vector", lambda e: e.tensor_copy(mub_sb[:], ps4[:, 0:512]))
        v_bc = tinc("vector", lambda e: e.tensor_copy(rsb_sb[:], ps5[:, 0:512]), sV)
        vself(v_bc)
        for dc in range(GPD):
            v = tinc("vector", lambda e, dc=dc: e.tensor_sub(
                sb_[0:128, dc * 512:(dc + 1) * 512],
                xb[0:128, dc * 512:(dc + 1) * 512], mub_sb[:]), sV)
            vself(v)
            v_x1 = tinc("vector", lambda e, dc=dc: e.tensor_mul(
                sb_[0:128, dc * 512:(dc + 1) * 512],
                sb_[0:128, dc * 512:(dc + 1) * 512], rsb_sb[:]), sV)
        P.wait("gpsimd", sV, v_bc)
        for dc in range(GPD, DC):
            p = tinc("gpsimd", lambda e, dc=dc: e.tensor_sub(
                sb_[0:128, dc * 512:(dc + 1) * 512],
                xb[0:128, dc * 512:(dc + 1) * 512], mub_sb[:]), sP)
            raw("gpsimd", lambda e, p=p: e.wait_ge(P.h(sP), p))
            p_x1 = tinc("gpsimd", lambda e, dc=dc: e.tensor_mul(
                sb_[0:128, dc * 512:(dc + 1) * 512],
                sb_[0:128, dc * 512:(dc + 1) * 512], rsb_sb[:]), sP)
        P.wait("sync", sV, v_x1)
        P.wait("sync", sP, p_x1)
        for dc in range(DC):
            dma("sync", x1_d[dc * 128:(dc + 1) * 128, sc * 512:(sc + 1) * 512],
                sb_[0:128, dc * 512:(dc + 1) * 512])
        st_val = P.cnt[dA]
        P.pending[dA] = max(P.pending[dA], st_val)
        P.wait("tensor", sV, v_x1)
        P.wait("tensor", sP, p_x1)
        for tt in range(4):
            for dc in range(DC):
                fn = lambda e, tt=tt, dc=dc: e.matmul(
                    ps4[:, 0:E],
                    sb_[0:128, dc * 512 + tt * 128: dc * 512 + tt * 128 + 128],
                    wg_sb[:, dc * E:(dc + 1) * E], start=(dc == 0), stop=(dc == DC - 1))
                if dc < DC - 1:
                    raw("tensor", fn)
                else:
                    t = tinc("tensor", fn, sT)
            P.wait("vector", sT, t)
            v = tinc("vector", lambda e: e.tensor_add(lgt_sb[:], ps4[:, 0:E], bgb_sb[:]), sV)
            vself(v)
            v = tinc("vector", lambda e: e.max(out=sm_sb[:, 0:8], in_=lgt_sb[:]), sV)
            vself(v)
            v = tinc("vector", lambda e: e.tensor_copy(sm_sb[:, 8:9], sm_sb[:, 1:2]), sV)
            vself(v)
            v_m = tinc("vector", lambda e: e.tensor_scalar(
                mtk_sb[:], lgt_sb[:], sm_sb[:, 8:9], None, op0=ALU.is_ge), sV)
            P.wait("tensor", sV, v_m)
            t = tinc("tensor", lambda e: e.transpose(ps4[0:16, 0:128], mtk_sb[:], iden_sb[:]), sT)
            P.wait("vector", sT, t)
            v = tinc("vector", lambda e, sc=sc, tt=tt: e.tensor_copy(
                mrows_sb[0:16, sc * 512 + tt * 128: sc * 512 + tt * 128 + 128],
                ps4[0:16, 0:128]), sV)
            P.wait("tensor", sV, v)
        cons_gate[b] = {"v": P.cnt[sV], "t": P.cnt[sT], "p": P.cnt[sP], "a": st_val}

    if CDB:
        c_loads(0)
        for sc in range(SC512):
            if sc + 1 < SC512:
                c_loads(sc + 1)
            c_compute(sc)
    else:
        for sc in range(SC512):
            c_loads(sc)
            c_compute(sc)

    mark("lnc")
    P.wait("sync", dG, P.cnt[dG])
    dma("sync", iota_sb[:], IOTA[0:16, 0:T])
    P.wait("vector", dA, P.cnt[dA])
    v = tinc("vector", lambda e: e.memset(zers_sb[:], 0.0), sV)
    vself(v)
    v = tinc("vector", lambda e: e.reduce_sum(sm_sb[0:16, 0:1], mrows_sb[:], axis=AX.X), sV)
    vself(v)
    v = tinc("vector", lambda e: e.tensor_scalar(
        sm_sb[0:16, 1:2], sm_sb[0:16, 0:1], -1.0, float(CAP), op0=ALU.mult, op1=ALU.add), sV)
    vself(v)
    v = tinc("vector", lambda e: e.tensor_tensor_scan(
        cums_sb[:], mrows_sb[:], zers_sb[:], 0.0, op0=ALU.add, op1=ALU.add), sV)
    vself(v)
    v = tinc("vector", lambda e: e.tensor_sub(cums_sb[:], iota_sb[:], cums_sb[:]), sV)
    vself(v)
    v = tinc("vector", lambda e: e.tensor_scalar(
        cums_sb[:], cums_sb[:], sm_sb[0:16, 1:2], None, op0=ALU.is_lt), sV)
    vself(v)
    v = tinc("vector", lambda e: e.tensor_scalar(
        zers_sb[:], mrows_sb[:], -1.0, 1.0, op0=ALU.mult, op1=ALU.add), sV)
    vself(v)
    v = tinc("vector", lambda e: e.tensor_tensor(cums_sb[:], cums_sb[:], zers_sb[:], ALU.min), sV)
    vself(v)
    v_fin = tinc("vector", lambda e: e.tensor_tensor(mrows_sb[:], mrows_sb[:], cums_sb[:], ALU.max), sV)

    P.wait("gpsimd", sV, v_fin)
    dma("gpsimd", m_d[0:16, 0:T], mrows_sb[:])
    P.wait("gpsimd", dG, P.cnt[dG])
    P.wait("gpsimd", dA, P.cnt[dA])

    mark("fill")
    for _eng in ("sync", "gpsimd", "vector"):
        for _sem in (sT, sV, sS, sP, dA, dG):
            P.wait(_eng, _sem, P.cnt[_sem])
    used_gate = [0, 0]
    for sp in range(NSP):
        q0 = sp * OSZ
        P.wait("gpsimd", sV, P.cnt[sV])
        P.wait("gpsimd", sT, P.cnt[sT])

        def _pull_own(e, q0=q0, gv=P.cnt[dG] + 16):
            e.reg_alu(REG["tmp"], REG["toff"], q0, ALU.add)
            e.dma_start(out=mt_sb[:],
                        in_=bass.AP(m_d, REG["tmp"], [[T, 16], [1, OSZ]])).then_inc(P.h(dG), 16)
            e.wait_ge(P.h(dG), gv)
            e.dma_start(out=bass.AP(arena, off_x1own, [[AR, 128], [OSZ, DC], [1, OSZ]]),
                        in_=bass.AP(x1_d, REG["tmp"], [[T, 128], [128 * T, DC], [1, OSZ]])
                        ).then_inc(P.h(dG), 16)
            e.dma_start(out=mown_d[0:16, 0:OSZ], in_=mt_sb[:]).then_inc(P.h(dG), 16)
        P.dma_barrier("gpsimd", dG)
        P.emit("gpsimd", _pull_own)
        P.bump(dG, 16 * 3)
        g_own = P.cnt[dG]

        P.wait("vector", dG, g_own)
        P.wait("vector", dA, P.cnt[dA])
        for dc in range(DC):
            fn = lambda e, dc=dc: e.memset(outs_sb[0:128, dc * OSZ:(dc + 1) * OSZ], 0.0)
            if dc < DC - 1:
                raw("vector", fn)
            else:
                v_outs0 = tinc("vector", fn, sV)
        vself(v_outs0)

        def prefetch(e_idx, buf, q0=q0):
            dma("gpsimd", w1_sb[buf][:], W1[e_idx, 0:128, 0:DC * FP])
            dma("gpsimd", w2_sb[buf][:], W2[e_idx, 0:128, 0:2 * D])

            def _mb(e, e_idx=e_idx, buf=buf):
                e.dma_start(out=mb_sb[buf][:],
                            in_=bass.AP(mown_d, e_idx * (T // 2), [[0, 128], [1, OSZ]])
                            ).then_inc(P.h(dG), 16)
            P.dma_barrier("gpsimd", dG)
            P.emit("gpsimd", _mb)
            P.bump(dG, 16)

        prefetch(0, 0)
        w_ready = [P.cnt[dG], 0]
        for ei in range(E):
            buf = ei % 2
            vself(P.cnt[sV])
            P.wait("tensor", dG, w_ready[buf])
            P.wait("tensor", sV, P.cnt[sV])
            if ei + 1 < E:
                if used_gate[(ei + 1) % 2]:
                    P.wait("gpsimd", sT, used_gate[(ei + 1) % 2])
                prefetch(ei + 1, (ei + 1) % 2)
                w_ready[(ei + 1) % 2] = P.cnt[dG]
            if ei == 0:
                P.wait("tensor", sV, v_outs0)
            gel_gate = {}
            for fc in range(2):
                for ts in range(NTS):
                    k = fc * NTS + ts
                    bank = ps0 if k % 2 == 0 else ps1
                    P.wait("tensor", sS, gel_gate.get(k - 2, 0))
                    for dc in range(DC):
                        fn = lambda e, buf=buf, fc=fc, ts=ts, dc=dc, bank=bank: e.matmul(
                            bank[:, 0:TS],
                            w1_sb[buf][0:128, dc * FP + fc * 128: dc * FP + fc * 128 + 128],
                            x1own_sb[0:128, dc * OSZ + ts * TS: dc * OSZ + ts * TS + TS],
                            start=(dc == 0), stop=(dc == DC - 1))
                        if dc < DC - 1:
                            raw("tensor", fn)
                        else:
                            t = tinc("tensor", fn, sT)
                    P.wait("scalar", sT, t)
                    s = tinc("scalar", lambda e, buf=buf, fc=fc, ts=ts, ei=ei, bank=bank: e.activation(
                        h_sb[0:128, fc * OSZ + ts * TS: fc * OSZ + ts * TS + TS], bank[:, 0:TS],
                        AF.Gelu, bias=b1c_sb[:, 2 * ei + fc: 2 * ei + fc + 1], scale=1.0), sS)
                    gel_gate[k] = s
                    P.wait("vector", sS, s)
                    P.wait("vector", dG, w_ready[buf])
                    fn = lambda e, buf=buf, fc=fc, ts=ts: e.tensor_mul(
                        h_sb[0:128, fc * OSZ + ts * TS: fc * OSZ + ts * TS + TS],
                        h_sb[0:128, fc * OSZ + ts * TS: fc * OSZ + ts * TS + TS],
                        mb_sb[buf][0:128, ts * TS: ts * TS + TS])
                    if fc == 1 and ts == NTS - 1:
                        v_h = tinc("vector", fn, sV)
                    else:
                        raw("vector", fn)
            P.wait("tensor", sV, v_h)
            addv = []
            for dc in range(DC):
                for ts in range(NTS):
                    k = dc * NTS + ts
                    bank = ps4 if k % 2 == 0 else ps5
                    if k >= 2:
                        P.wait("tensor", sV, addv[k - 2])
                    for fc in range(2):
                        fn = lambda e, buf=buf, dc=dc, ts=ts, fc=fc, bank=bank: e.matmul(
                            bank[:, 0:TS],
                            w2_sb[buf][0:128, fc * D + dc * 128: fc * D + dc * 128 + 128],
                            h_sb[0:128, fc * OSZ + ts * TS: fc * OSZ + ts * TS + TS],
                            start=(fc == 0), stop=(fc == 1))
                        if fc == 0:
                            raw("tensor", fn)
                        else:
                            t = tinc("tensor", fn, sT)
                    P.wait("vector", sT, t)
                    addv.append(tinc("vector", lambda e, dc=dc, ts=ts, bank=bank: e.tensor_add(
                        outs_sb[0:128, dc * OSZ + ts * TS: dc * OSZ + ts * TS + TS],
                        bank[:, 0:TS],
                        outs_sb[0:128, dc * OSZ + ts * TS: dc * OSZ + ts * TS + TS]), sV))
            used_gate[buf] = P.cnt[sT]
        vself(P.cnt[sV])
        P.wait("tensor", dG, g_own)
        P.wait("tensor", sV, P.cnt[sV])
        addv = []
        for dc in range(DC):
            for ts in range(NTS):
                k = dc * NTS + ts
                bank = ps4 if k % 2 == 0 else ps5
                if k >= 2:
                    P.wait("tensor", sV, addv[k - 2])
                t = tinc("tensor", lambda e, dc=dc, ts=ts, bank=bank: e.matmul(
                    bank[:, 0:TS], b2_sb[:, dc * 128:(dc + 1) * 128],
                    mt_sb[0:16, ts * TS: ts * TS + TS], start=True, stop=True), sT)
                P.wait("vector", sT, t)
                addv.append(tinc("vector", lambda e, dc=dc, ts=ts, bank=bank: e.tensor_add(
                    outs_sb[0:128, dc * OSZ + ts * TS: dc * OSZ + ts * TS + TS],
                    bank[:, 0:TS],
                    outs_sb[0:128, dc * OSZ + ts * TS: dc * OSZ + ts * TS + TS]), sV))

        vself(P.cnt[sV])
        P.wait("vector", sT, P.cnt[sT])
        for dc in range(DC):
            fn = lambda e, dc=dc: e.tensor_add(
                outs_sb[0:128, dc * OSZ:(dc + 1) * OSZ],
                outs_sb[0:128, dc * OSZ:(dc + 1) * OSZ],
                x1own_sb[0:128, dc * OSZ:(dc + 1) * OSZ])
            if dc < DC - 1:
                raw("vector", fn)
            else:
                v_z2 = tinc("vector", fn, sV)

        def emit_ln(src_sb, dst_sb, affine, resid_sb, v_ready):
            vv = v_ready
            for ts in range(NTS):
                P.wait("scalar", sV, vv)
                P.wait("tensor", sV, vv)
                for dc in range(DC):
                    fn = lambda e, dc=dc, ts=ts: e.matmul(
                        ps4[0:1, 0:TS], onesc_sb[:],
                        src_sb[0:128, dc * OSZ + ts * TS: dc * OSZ + ts * TS + TS],
                        start=(dc == 0), stop=(dc == DC - 1))
                    if dc < DC - 1:
                        raw("tensor", fn)
                    else:
                        tinc("tensor", fn, sT)
                sq_mm = {}
                for dc in range(DC):
                    sqb = sq_sb if dc % 2 == 0 else sq2_sb
                    if dc >= 2:
                        P.wait("scalar", sT, sq_mm[dc - 2])
                    s = tinc("scalar", lambda e, dc=dc, ts=ts, sqb=sqb: e.activation(
                        sqb[:, 0:TS],
                        src_sb[0:128, dc * OSZ + ts * TS: dc * OSZ + ts * TS + TS],
                        AF.Square, bias=0.0, scale=1.0), sS)
                    P.wait("tensor", sS, s)
                    sq_mm[dc] = tinc("tensor", lambda e, dc=dc, sqb=sqb: e.matmul(
                        ps5[0:1, 0:TS], onesc_sb[:], sqb[:, 0:TS],
                        start=(dc == 0), stop=(dc == DC - 1)), sT)
                P.wait("vector", sT, P.cnt[sT])
                raw("vector", lambda e: e.tensor_scalar_mul(row_sb[:, 0:TS], ps4[0:1, 0:TS], 1.0 / D))
                v = tinc("vector", lambda e: e.tensor_scalar_mul(row_sb[:, 512:512 + TS], ps5[0:1, 0:TS], 1.0 / D), sV)
                vself(v)
                v = tinc("vector", lambda e: e.tensor_tensor(
                    sq_sb[0:1, 0:TS], row_sb[:, 0:TS], row_sb[:, 0:TS], ALU.mult), sV)
                vself(v)
                v = tinc("vector", lambda e: e.tensor_sub(
                    row_sb[:, 512:512 + TS], row_sb[:, 512:512 + TS], sq_sb[0:1, 0:TS]), sV)
                P.wait("scalar", sV, v)
                s = tinc("scalar", lambda e: e.activation(
                    row_sb[:, 512:512 + TS], row_sb[:, 512:512 + TS], AF.Sqrt,
                    bias=eps_sb[0:1, 0:1], scale=1.0), sS)
                P.wait("vector", sS, s)
                v_rs = tinc("vector", lambda e: e.reciprocal(row_sb[:, 512:512 + TS], row_sb[:, 512:512 + TS]), sV)
                P.wait("tensor", sV, v_rs)
                raw("tensor", lambda e: e.matmul(ps4[:, 0:TS], ones1_sb[:], row_sb[:, 0:TS],
                                                 start=True, stop=True))
                t = tinc("tensor", lambda e: e.matmul(ps5[:, 0:TS], ones1_sb[:], row_sb[:, 512:512 + TS],
                                                      start=True, stop=True), sT)
                P.wait("vector", sT, t)
                raw("vector", lambda e: e.tensor_copy(mub_sb[:, 0:TS], ps4[:, 0:TS]))
                v = tinc("vector", lambda e: e.tensor_copy(rsb_sb[:, 0:TS], ps5[:, 0:TS]), sV)
                vself(v)
                for dc in range(DC):
                    v = tinc("vector", lambda e, dc=dc, ts=ts: e.tensor_sub(
                        tq_sb[:, 0:TS],
                        src_sb[0:128, dc * OSZ + ts * TS: dc * OSZ + ts * TS + TS],
                        mub_sb[:, 0:TS]), sV)
                    vself(v)
                    v = tinc("vector", lambda e: e.tensor_mul(
                        tq_sb[:, 0:TS], tq_sb[:, 0:TS], rsb_sb[:, 0:TS]), sV)
                    vself(v)
                    if affine:
                        v = tinc("vector", lambda e, dc=dc: e.tensor_scalar(
                            tq_sb[:, 0:TS], tq_sb[:, 0:TS], gam_sb[:, dc:dc + 1],
                            bet_sb[:, dc:dc + 1], op0=ALU.mult, op1=ALU.add), sV)
                        vself(v)
                    if resid_sb is not None:
                        fn = lambda e, dc=dc, ts=ts: e.tensor_add(
                            dst_sb[0:128, dc * OSZ + ts * TS: dc * OSZ + ts * TS + TS],
                            tq_sb[:, 0:TS],
                            resid_sb[0:128, dc * OSZ + ts * TS: dc * OSZ + ts * TS + TS])
                    else:
                        fn = lambda e, dc=dc, ts=ts: e.tensor_copy(
                            dst_sb[0:128, dc * OSZ + ts * TS: dc * OSZ + ts * TS + TS],
                            tq_sb[:, 0:TS])
                    vv = tinc("vector", fn, sV)
                    vself(vv)
            return vv

        v_z3 = emit_ln(outs_sb, x1own_sb, True, x1own_sb, v_z2)
        v_out = emit_ln(x1own_sb, outs_sb, False, None, v_z3)

        P.wait("sync", sV, v_out)
        for dc in range(DC):
            dma("sync", out_fm[dc * 128:(dc + 1) * 128, q0:q0 + OSZ],
                outs_sb[0:128, dc * OSZ:(dc + 1) * OSZ])

    P.wait("sync", dA, P.cnt[dA])
    mark("all")

    if upto is not None:
        cut = P.marks[upto]
        for e in ENGS:
            P.q[e] = P.q[e][:cut[e]]

    with ctx:
        with nc.Block() as block:
            @block.sync
            def _(e):
                for f in P.q["sync"]:
                    f(e)

            @block.gpsimd
            def _(e):
                for f in P.q["gpsimd"]:
                    f(e)

            @block.tensor
            def _(e):
                for f in P.q["tensor"]:
                    f(e)

            @block.vector
            def _(e):
                for f in P.q["vector"]:
                    f(e)

            @block.scalar
            def _(e):
                for f in P.q["scalar"]:
                    f(e)

    return nc


def prep_inputs(inputs, T=4096):
    x = np.asarray(inputs["L0_attn_x"], np.float32)
    W_dq = np.asarray(inputs["L0_attn_W_dq"], np.float32)
    W_uq = np.asarray(inputs["L0_attn_W_uq"], np.float32)
    W_dkv = np.asarray(inputs["L0_attn_W_dkv"], np.float32)
    W_uk = np.asarray(inputs["L0_attn_W_uk"], np.float32)
    W_uv = np.asarray(inputs["L0_attn_W_uv"], np.float32)
    W_o = np.asarray(inputs["L0_attn_W_o"], np.float32)
    wg = np.asarray(inputs["L0_ffn_wg"], np.float32)
    bg = np.asarray(inputs["L0_ffn_bg"], np.float32)
    w1 = np.asarray(inputs["L0_ffn_w1"], np.float32)
    b1 = np.asarray(inputs["L0_ffn_b1"], np.float32)
    w2 = np.asarray(inputs["L0_ffn_w2"], np.float32)
    b2 = np.asarray(inputs["L0_ffn_b2"], np.float32)
    gamma = np.asarray(inputs["L0_ffn_gamma"], np.float32)
    beta = np.asarray(inputs["L0_ffn_beta"], np.float32)

    def cm(a):
        K = a.shape[1]
        return np.ascontiguousarray(a.reshape(DC, 128, K).transpose(1, 0, 2).reshape(128, DC * K))

    shared = {
        "Wdkv": cm(W_dkv.T), "Wdq": cm(W_dq.T), "Wuq": cm(W_uq),
        "Wuk": cm(W_uk), "Wuv": cm(W_uv), "WG": cm(wg),
        "BGB": np.ascontiguousarray(np.tile(bg[None, :], (128, 1))),
        "W1": np.ascontiguousarray(w1.reshape(E, DC, 128, FP).transpose(0, 2, 1, 3).reshape(E, 128, DC * FP)),
        "B1C": np.ascontiguousarray(b1.reshape(E, 2, 128).transpose(2, 0, 1).reshape(128, 2 * E)),
        "W2": np.ascontiguousarray(w2.reshape(E, 2, 128, D).transpose(0, 2, 1, 3).reshape(E, 128, 2 * D)),
        "B2": np.ascontiguousarray(b2),
        "GAM": np.ascontiguousarray(gamma.reshape(DC, 128).T),
        "BET": np.ascontiguousarray(beta.reshape(DC, 128).T),
        "IDEN": np.eye(128, dtype=np.float32),
        "TRIL": np.where(np.arange(128)[None, :] <= np.arange(128)[:, None],
                         0.0, NEG).astype(np.float32),
        "IOTA": np.ascontiguousarray(np.tile(np.arange(T, dtype=np.float32)[None, :], (16, 1))),
    }
    WoT = W_o.T
    WuqT = W_uq.T
    in_maps = []
    for i in range(8):
        b, p = i // 2, i % 2
        m = dict(shared)
        m["xT"] = np.ascontiguousarray(x[b].T)
        m["Wdqmy"] = np.ascontiguousarray(W_dq[:, 512 * p:512 * p + 512])
        m["Wuqtmy"] = np.ascontiguousarray(WuqT[:, 512 * p:512 * p + 512])
        m["Wotmy"] = cm(WoT[:, 512 * p:512 * p + 512])
        m["TOFF"] = np.array([[T // 2 * p]], np.int32)
        in_maps.append(m)
    return in_maps


def kernel(**inputs):
    T = int(np.asarray(inputs["L0_attn_x"]).shape[1])
    in_maps = prep_inputs(inputs, T)
    nc = build(T)
    res = run_bass_kernel_spmd(nc, in_maps, core_ids=list(range(8)))
    B = int(np.asarray(inputs["L0_attn_x"]).shape[0])
    out = np.empty((B, T, D), np.float32)
    OWN = T // 2
    for i in range(8):
        b, p = i // 2, i % 2
        out[b, OWN * p: OWN * (p + 1), :] = np.asarray(res.results[i]["out_fm"]).T
    return out
